# version 37
# baseline (speedup 1.0000x reference)
import contextlib
import math
import numpy as np
import ml_dtypes
import concourse.bass as bass
import concourse.mybir as mybir
from concourse.bass_utils import run_bass_kernel_spmd

F32 = mybir.dt.float32
BF16 = mybir.dt.bfloat16
I32 = mybir.dt.int32
U32 = mybir.dt.uint32
U8 = mybir.dt.uint8
AF = mybir.ActivationFunctionType
ALU = mybir.AluOpType
AXX = mybir.AxisListType.X
AX = mybir.AxisListType
ds = bass.ds

B, N, D, H, HD, HID = 4, 4096, 1024, 16, 64, 3072
M = B * N
NCORE = 8
TOWN = M // NCORE
KD = D // 128
SCALE = 0.125
CMARGIN = math.log(0.125 * 1.1)
FSC = 1.0 / 127.0
NEG = -1.0e30

O_Q = 0
O_K = 8192
O_VAUG = 16384
O_XG0 = 24832
O_XG1 = 33024
O_AVST = 41216
O_SUMS = 57600
O_RECIP = 73984
O_ATTNU = 90368
O_AT0 = 0
O_AT1 = 8192
O_WOUT = 16384
O_XQ = 32768
O_XS = 49152
O_XSQ = 65536
O_TMPF = 73984
O_X1 = 90368
O_DACC = 0
O_WG = 32768
O_WU = 45056
O_WD = 57344
O_H = 69632
O_PSTG = 98560
O_XSQ2 = 123136
O_TMPF2 = 131328
POOL = 147712

DEBUG = False
DBG_UNIT = 3


def build_nc():
    nc = bass.Bass("TRN2", target_bir_lowering=False,
                   detect_race_conditions=False)

    xg = nc.dram_tensor("xg", [D, M], F32, kind="ExternalInput").ap()
    xq = nc.dram_tensor("xq", [D, TOWN], F32, kind="ExternalInput").ap()
    wqkv = nc.dram_tensor("wqkv", [D, 384], F32, kind="ExternalInput").ap()
    bqkv = nc.dram_tensor("bqkv", [128, 3], F32, kind="ExternalInput").ap()
    woutc = nc.dram_tensor("woutc", [128, D], F32, kind="ExternalInput").ap()
    bout = nc.dram_tensor("bout", [128, KD], F32, kind="ExternalInput").ap()
    wg = nc.dram_tensor("wg", [D, HID], I32, kind="ExternalInput").ap()
    wu = nc.dram_tensor("wu", [D, HID], I32, kind="ExternalInput").ap()
    wd = nc.dram_tensor("wd", [HID, D], I32, kind="ExternalInput").ap()
    out = nc.dram_tensor("out", [D, TOWN], F32, kind="ExternalOutput").ap()
    dbg = {}
    if DEBUG:
        for nm, shape, dt in [
            ("Q", [65, 4096], BF16), ("K", [65, 4096], BF16),
            ("VA", [128, 4160], BF16), ("RS", [128, 64], F32),
            ("CR", [64, 128], BF16), ("SUMS", [33, 4096], F32),
            ("RECIP", [33, 4096], F32), ("AVST", [128, 4096], F32),
            ("ATTNU", [128, 4096], BF16), ("PSTG", [128, 4096], BF16),
            ("XS", [128, 4096], F32), ("X1", [128, 16384], BF16),
            ("DACC", [128, 8192], F32), ("HB", [128, 3072], BF16),
            ("ATO3", [128, 4096], BF16),
            ("DACC0", [128, 8192], F32), ("STAT", [1, 2048], F32),
            ("XS0", [128, 4096], F32), ("ATO0", [128, 4096], BF16),
            ("HB0", [128, 3072], BF16),
        ]:
            dbg[nm] = nc.dram_tensor("dbg_" + nm, shape, dt,
                                     kind="ExternalOutput").ap()

    prs = [nc.dram_tensor("prs%d" % u, [NCORE * D, 512], BF16, kind="Internal").ap()
           for u in range(B)]
    crow_d = nc.dram_tensor("crow_d", [64, 128], BF16, kind="Internal").ap()
    rsout = [nc.dram_tensor("rsout%d" % u, [D, 512], BF16, kind="Internal").ap()
             for u in range(B)]

    ident_c = nc.inline_tensor(np.eye(128, dtype=ml_dtypes.bfloat16), "identc").ap()
    tri = np.zeros((128, 128), np.float32)
    for p in range(128):
        tri[p, :p] = NEG
    mtri_c = nc.inline_tensor(tri, "mtric").ap()
    triu = np.zeros((128, 128), np.float32)
    for p in range(128):
        triu[p, p + 1:] = NEG
    mtriu_c = nc.inline_tensor(triu, "mtriuc").ap()
    sel = np.zeros((33, 128), np.float32)
    sel[0, 0:64] = 1.0
    sel[32, 64:128] = 1.0
    sel_c = nc.inline_tensor(sel, "selc").ap()
    ones_f_c = nc.inline_tensor(np.ones((128, 1), np.float32), "onesfc").ap()
    ones_b_c = nc.inline_tensor(np.ones((128, 1), dtype=ml_dtypes.bfloat16), "onesbc").ap()
    onesrow_f_c = nc.inline_tensor(np.ones((1, 128), np.float32), "onesrowfc").ap()
    negrow_b_c = nc.inline_tensor(np.full((1, 128), -1.0, dtype=ml_dtypes.bfloat16), "negrowbc").ap()

    xg_v = xg.rearrange("(k p) t -> p k t", p=128)
    xq_v = xq.rearrange("(k p) t -> p k t", p=128)
    wqkv_v = wqkv.rearrange("(k p) f -> p k f", p=128)
    wg_v = wg.rearrange("(k p) f -> p k f", p=128)
    wu_v = wu.rearrange("(k p) f -> p k f", p=128)
    wd_v = wd.rearrange("(i p) f -> p i f", p=128)
    prs_v = [t.rearrange("(j f p) t -> p j f t", p=128, f=KD) for t in prs]
    rsout_v = [t.rearrange("(f p) t -> p f t", p=128) for t in rsout]
    out_v = out.rearrange("(k p) t -> p k t", p=128)

    sem_names = ["pe", "act", "dve", "pool", "cc", "g", "s", "gx0", "gx1",
                 "ga0", "ga1", "gq", "gw", "gd", "gag", "so", "dbg"]

    stack = contextlib.ExitStack()
    pool = stack.enter_context(nc.sbuf_tensor("pool", [128, POOL], U8))
    wqkv_sb = stack.enter_context(nc.sbuf_tensor("wqkv_sb", [128, KD, 384], BF16))
    bqkv_sb = stack.enter_context(nc.sbuf_tensor("bqkv_sb", [128, 3], F32))
    bout_sb = stack.enter_context(nc.sbuf_tensor("bout_sb", [128, KD], F32))
    ident_sb = stack.enter_context(nc.sbuf_tensor("ident_sb", [128, 128], BF16))
    mtri_sb = stack.enter_context(nc.sbuf_tensor("mtri_sb", [128, 128], F32))
    sel_sb = stack.enter_context(nc.sbuf_tensor("sel_sb", [33, 128], F32))
    onesf_sb = stack.enter_context(nc.sbuf_tensor("onesf_sb", [128, 1], F32))
    onesb_sb = stack.enter_context(nc.sbuf_tensor("onesb_sb", [128, 1], BF16))
    onesrowf_sb = stack.enter_context(nc.sbuf_tensor("onesrowf_sb", [1, 128], F32))
    negrowb_sb = stack.enter_context(nc.sbuf_tensor("negrowb_sb", [1, 128], BF16))
    vf_sb = stack.enter_context(nc.sbuf_tensor("vf_sb", [128, 2, 512], BF16))
    QA_sb = stack.enter_context(nc.sbuf_tensor("QA_sb", [65, 4096], BF16))
    QB_sb = stack.enter_context(nc.sbuf_tensor("QB_sb", [65, 4096], BF16))
    KA_sb = stack.enter_context(nc.sbuf_tensor("KA_sb", [65, 4096], BF16))
    KB_sb = stack.enter_context(nc.sbuf_tensor("KB_sb", [65, 4096], BF16))
    scr8_sb = stack.enter_context(nc.sbuf_tensor("scr8_sb", [128, 2, 512], BF16))
    acc8_sb = stack.enter_context(nc.sbuf_tensor("acc8_sb", [128, 4], F32))
    runsum_sb = stack.enter_context(nc.sbuf_tensor("runsum_sb", [128, 2, 32], F32))
    runln_sb = stack.enter_context(nc.sbuf_tensor("runln_sb", [128, 64], BF16))
    crow_sb = stack.enter_context(nc.sbuf_tensor("crow_sb", [64, 128], BF16))
    mtriu_sb = stack.enter_context(nc.sbuf_tensor("mtriu_sb", [128, 128], F32))
    alphat_sb = stack.enter_context(nc.sbuf_tensor("alphat_sb", [128, 2, 1024], BF16))
    silu_sb = stack.enter_context(nc.sbuf_tensor("silu_sb", [128, 2, 512], BF16))
    statrow_sb = stack.enter_context(nc.sbuf_tensor("statrow_sb", [1, 4, 512], F32))
    woutc_sb = stack.enter_context(nc.sbuf_tensor("woutc_sb", [128, 1024], BF16))

    ps_s0 = stack.enter_context(nc.psum_tensor("ps_s0", [128, 1024], F32))
    ps_s1 = stack.enter_context(nc.psum_tensor("ps_s1", [128, 1024], F32))
    ps_a0 = stack.enter_context(nc.psum_tensor("ps_a0", [128, 512], F32))
    ps_a1 = stack.enter_context(nc.psum_tensor("ps_a1", [128, 512], F32))
    ps_t0 = stack.enter_context(nc.psum_tensor("ps_t0", [128, 128], BF16))
    ps_t1 = stack.enter_context(nc.psum_tensor("ps_t1", [128, 128], BF16))

    sems = {nm: stack.enter_context(nc.semaphore("sem_" + nm)) for nm in sem_names}

    def pv(off, dtype, cols, rows=128):
        nbytes = cols * (2 if dtype == BF16 else 4)
        ap = pool[0:rows, off:off + nbytes].bitcast(dtype)
        assert ap.shape == (rows, cols), (ap.shape, rows, cols)
        return ap

    VA_sb = pv(O_VAUG, BF16, 4160)
    XG = [pv(O_XG0, BF16, 4096), pv(O_XG1, BF16, 4096)]
    AVST = pv(O_AVST, F32, 4096)
    SUMS = pv(O_SUMS, F32, 4096, rows=33)
    RECIP = pv(O_RECIP, F32, 4096, rows=33)
    ATTNU = pv(O_ATTNU, BF16, 4096)
    ATO = [pv(O_AT0, BF16, 4096), pv(O_AT1, BF16, 4096)]
    XQS = pv(O_XQ, F32, 4096)
    XS = pv(O_XS, F32, 4096)
    XSB = [XS, pv(O_WOUT, F32, 4096)]
    XSQ = pv(O_XSQ, BF16, 4096)
    TMPF = pv(O_TMPF, F32, 4096)
    X1 = pv(O_X1, BF16, 16384)
    DACC = pv(O_DACC, F32, 8192)
    WGS = pv(O_WG, BF16, 6144)
    WUS = pv(O_WU, BF16, 6144)
    WDS = pv(O_WD, BF16, 6144)
    HB = pv(O_H, BF16, 3072)
    PSTG = pv(O_PSTG, BF16, 4096)
    XSQ2 = pv(O_XSQ2, BF16, 4096)
    TMPF2 = pv(O_TMPF2, F32, 4096)

    def r3(ap, k, inner):
        return ap.rearrange("p (k t) -> p k t", k=k)

    def emit(cur, eng):
        c = {nm: 0 for nm in sem_names}
        m = {}

        def on(name):
            return cur == name

        def step(sem, inst, n=1):
            c[sem] += n
            if inst is not None:
                inst.then_inc(sems[sem], n)
            return c[sem]

        def wait(target_eng, sem, val):
            if val is None or val <= 0:
                return
            if on(target_eng):
                eng.wait_ge(sems[sem], val)

        def T(fn):
            return fn(eng) if on("T") else None

        def A(fn):
            return fn(eng) if on("A") else None

        def V(fn):
            return fn(eng) if on("V") else None

        def G(fn):
            return fn(eng) if on("G") else None

        def S(fn):
            return fn(eng) if on("S") else None

        def dump(nm, src_ap, gates):
            if not DEBUG:
                return
            for semname, val in gates:
                wait("S", semname, val)
            step("dbg", S(lambda e: e.dma_start(out=dbg[nm], in_=src_ap)), 16)

        step("s", S(lambda e: e.dma_start(out=ident_sb[:], in_=ident_c)), 16)
        step("s", S(lambda e: e.dma_start(out=mtri_sb[:], in_=mtri_c)), 16)
        step("s", S(lambda e: e.dma_start(out=mtriu_sb[:], in_=mtriu_c)), 16)
        step("s", S(lambda e: e.dma_start(out=sel_sb[:], in_=sel_c)), 16)
        step("s", S(lambda e: e.dma_start(out=onesf_sb[:], in_=ones_f_c)), 16)
        step("s", S(lambda e: e.dma_start(out=onesb_sb[:], in_=ones_b_c)), 16)
        step("s", S(lambda e: e.dma_start(out=onesrowf_sb[:], in_=onesrow_f_c)), 16)
        step("s", S(lambda e: e.dma_start(out=negrowb_sb[:], in_=negrow_b_c)), 16)
        step("s", S(lambda e: e.dma_start(out=bqkv_sb[:], in_=bqkv)), 16)
        step("s", S(lambda e: e.dma_start(out=bout_sb[:], in_=bout)), 16)
        m["s_setup"] = c["s"]
        step("g", G(lambda e: e.dma_start(out=wqkv_sb[:], in_=wqkv_v)), 16)
        step("g", G(lambda e: e.dma_start(out=woutc_sb[:], in_=woutc)), 16)
        m["g_wqkv"] = c["g"]
        step("gx0", G(lambda e: e.dma_start(out=r3(XG[0], KD, 512),
                                            in_=xg_v[:, :, 0:512])), 16)
        m["xg", 0, 0] = c["gx0"]
        step("pool", G(lambda e: e.memset(VA_sb[:], 1.0)))
        step("pool", G(lambda e: e.memset(SUMS[:], 1.0)))
        step("pool", G(lambda e: e.memset(KA_sb[64:65, :], 1.0)))
        step("pool", G(lambda e: e.memset(KB_sb[64:65, :], 1.0)))
        m["pool_init"] = c["pool"]
        wait("T", "s", m["s_setup"])
        wait("A", "s", m["s_setup"])
        wait("V", "s", m["s_setup"])
        def a1_mms(u, s):
                xbuf = XG[s % 2]
                xv = r3(xbuf, KD, 512)
                ns_u, ns_s = (u, s + 1) if s < 7 else (u + 1, 0)
                if ns_u < B:
                    nb = (s + 1) % 2
                    if ("qkv_pe", u, s - 1) in m:
                        wait("G", "pe", m["qkv_pe", u, s - 1])
                    elif u > 0 and s == 0:
                        wait("G", "pe", m["qkv_pe", u - 1, 7])
                    gx = "gx" + str(nb)
                    off = ns_u * N + ns_s * 512
                    inst = G(lambda e, nb=nb, off=off: e.dma_start(
                        out=r3(XG[nb], KD, 512), in_=xg_v[:, :, off:off + 512]))
                    m["xg", ns_u, ns_s] = step(gx, inst, 16)

                wait("T", "gx" + str(s % 2), m["xg", u, s])
                wait("T", "g", m["g_wqkv"])
                if u > 0 and s == 0:
                    wait("T", "pe", m["avlast", u - 1])
                    wait("T", "act", m["explast", u - 1])
                if ("exp8prog", u) in m:
                    wait("T", "dve", m["exp8prog", u])
                if ("qkv_evac4", u, s - 1) in m:
                    wait("T", "act", m["qkv_evac4", u, s - 1])
                for k in range(KD):
                    T(lambda e, k=k: e.matmul(
                        ps_s0[:, 0:512], wqkv_sb[:, k, 0:128], xv[:, k, :],
                        start=(k == 0), stop=(k == KD - 1)))
                for k in range(KD):
                    T(lambda e, k=k: e.matmul(
                        ps_s0[:, 512:1024], wqkv_sb[:, k, 128:256], xv[:, k, :],
                        start=(k == 0), stop=(k == KD - 1)))
                if ("qkv_evac", u, s - 1) in m:
                    wait("T", "act", m["qkv_evac", u, s - 1])
                inst = None
                for k in range(KD):
                    inst = T(lambda e, k=k: e.matmul(
                        ps_s1[:, 0:512], wqkv_sb[:, k, 256:384], xv[:, k, :],
                        start=(k == 0), stop=(k == KD - 1)))
                m["qkv_pe", u, s] = step("pe", inst)

                wait("A", "pe", m["qkv_pe", u, s])
                if u > 0 and s == 0:
                    wait("A", "pe", m["avlast", u - 1])
                A(lambda e, s=s: e.activation(QA_sb[0:64, s * 512:(s + 1) * 512],
                                              ps_s0[0:64, 0:512], AF.Identity,
                                              bias=bqkv_sb[0:64, 0:1]))
                A(lambda e, s=s: e.activation(QB_sb[0:64, s * 512:(s + 1) * 512],
                                              ps_s0[64:128, 0:512], AF.Identity,
                                              bias=bqkv_sb[64:128, 0:1]))
                A(lambda e, s=s: e.activation(KA_sb[0:64, s * 512:(s + 1) * 512],
                                              ps_s0[0:64, 512:1024], AF.Identity,
                                              bias=bqkv_sb[0:64, 1:2]))
                inst = A(lambda e, s=s: e.activation(KB_sb[0:64, s * 512:(s + 1) * 512],
                                              ps_s0[64:128, 512:1024], AF.Identity,
                                              bias=bqkv_sb[64:128, 1:2]))
                m["qkv_evac4", u, s] = step("act", inst)
                if ("vtr_pe", u, s - 2) in m:
                    wait("A", "pe", m["vtr_pe", u, s - 2])
                inst = A(lambda e, s=s: e.activation(vf_sb[:, s % 2, :],
                                                     ps_s1[:, 0:512], AF.Identity,
                                                     bias=bqkv_sb[:, 2:3]))
                m["qkv_evac", u, s] = step("act", inst)

        def a1_vtr(u, s):
                wait("T", "act", m["qkv_evac", u, s])
                for tb in range(4):
                    pst = [ps_t0, ps_t1][tb % 2]
                    if ("vcp", u, s, tb - 2) in m:
                        wait("T", "dve", m["vcp", u, s, tb - 2])
                    elif ("vcp", u, s - 1, tb + 2) in m:
                        wait("T", "dve", m["vcp", u, s - 1, tb + 2])
                    elif u > 0 and s == 0:
                        wait("T", "dve", m["ccp", u - 1])
                    inst = T(lambda e, s=s, tb=tb, pst=pst: e.transpose(
                        pst[:], vf_sb[:, s % 2, tb * 128:(tb + 1) * 128], ident_sb[:]))
                    m["vtr_pe_i", u, s, tb] = step("pe", inst)
                    wait("V", "pe", m["vtr_pe_i", u, s, tb])
                    if u == 0 and s == 0 and tb == 0:
                        wait("V", "pool", m["pool_init"])
                    blk = s * 4 + tb
                    dst = VA_sb[:, blk * 130:(blk + 1) * 130].rearrange(
                        "p (g c) -> p g c", g=2)[:, :, 0:64]
                    src = pst[:].rearrange("p (g c) -> p g c", g=2)
                    inst = V(lambda e, dst=dst, src=src: e.tensor_copy(dst, src))
                    m["vcp", u, s, tb] = step("dve", inst)
                m["vtr_pe", u, s] = m["vtr_pe_i", u, s, 3]

        def pre_block(u, qb, gate_s):
                pj = pjc[u]
                nkt = (qb * 128) // 512 + 1
                scr8f = scr8_sb[:].rearrange("p h c -> p (h c)").bitcast(F32)
                for kt in range(nkt):
                    kN = min(512, (qb + 1) * 128 - kt * 512)
                    psq = [ps_s0, ps_s1][pj % 2]
                    if ("rmax8", u, pj - 2) in m:
                        wait("T", "dve", m["rmax8", u, pj - 2])
                    if kt == 0:
                        wait("T", "act", m["qkv_evac", u, gate_s])
                    if pj == 0 and u == 0:
                        wait("T", "pool", m["pool_init"])
                    qsl = slice(qb * 128, (qb + 1) * 128)
                    ksl = slice(kt * 512, kt * 512 + kN)
                    T(lambda e, psq=psq, qsl=qsl, ksl=ksl, kN=kN: e.matmul(
                        psq[:, 0:kN], QA_sb[0:64, qsl], KA_sb[0:64, ksl],
                        start=True, stop=True))
                    inst = T(lambda e, psq=psq, qsl=qsl, ksl=ksl, kN=kN: e.matmul(
                        psq[:, 512:512 + kN], QB_sb[0:64, qsl], KB_sb[0:64, ksl],
                        start=True, stop=True))
                    m["sc8", u, pj] = step("pe", inst)
                    diag = (kt == nkt - 1)
                    wait("V", "pe", m["sc8", u, pj])
                    if kt == 0 and ("rmaxq", u, qb - 1) in m:
                        wait("V", "dve", m["rmaxq", u, qb - 1])
                    if diag:
                        dcol = qb * 128 - kt * 512
                        V(lambda e, psq=psq, dcol=dcol: e.tensor_tensor(
                            psq[:, dcol:dcol + 128], psq[:, dcol:dcol + 128],
                            mtriu_sb[:], ALU.add))
                        inst = V(lambda e, psq=psq, dcol=dcol: e.tensor_tensor(
                            psq[:, 512 + dcol:512 + dcol + 128],
                            psq[:, 512 + dcol:512 + dcol + 128],
                            mtriu_sb[:], ALU.add))
                        m["msk8", u, pj] = step("dve", inst)
                        wait("V", "dve", m["msk8", u, pj])
                    V(lambda e, psq=psq, kN=kN, kt=kt: e.reduce_max(
                        scr8f[:, kt:kt + 1], psq[:, 0:kN], AXX))
                    inst = V(lambda e, psq=psq, kN=kN, kt=kt: e.reduce_max(
                        scr8f[:, 256 + kt:256 + kt + 1], psq[:, 512:512 + kN], AXX))
                    m["rmax8", u, pj] = step("dve", inst)
                    pj += 1

                wait("V", "dve", m["rmax8", u, pj - 1])
                V(lambda e, qb=qb, nkt=nkt: e.reduce_max(
                    runsum_sb[:, 0, qb:qb + 1], scr8f[:, 0:nkt], AXX))
                inst = V(lambda e, qb=qb, nkt=nkt: e.reduce_max(
                    runsum_sb[:, 1, qb:qb + 1], scr8f[:, 256:256 + nkt], AXX))
                m["rmaxq", u, qb] = step("dve", inst)

                pjc[u] = pj
                m['exp8prog', u] = m["rmax8", u, pj - 1]
        def crow(u):
            pj = pjc[u]
            wait("V", "dve", m["rmaxq", u, 31])
            inst = V(lambda e: e.tensor_scalar(runln_sb[:], runsum_sb[:].rearrange(
                "p h q -> p (h q)"), -1.0, None, ALU.mult))
            m["cneg", u] = step("dve", inst)
            wait("T", "dve", m["cneg", u])
            inst = T(lambda e: e.transpose(ps_t0[0:64, :], runln_sb[:], ident_sb[:]))
            m["ctr", u] = step("pe", inst)
            wait("V", "pe", m["ctr", u])
            inst = V(lambda e: e.tensor_copy(crow_sb[:], ps_t0[0:64, :]))
            m["ccp", u] = step("dve", inst)
            wait("G", "dve", m["ccp", u])
            inst = G(lambda e: e.dma_start(out=crow_d, in_=crow_sb[:]))
            step("gag", inst, 16)
            wait("G", "gag", c["gag"])
            cv = crow_d.rearrange("(h q) t -> h (q t)", h=2)
            inst = G(lambda e: e.dma_start(out=QA_sb[64:65, :], in_=cv[0:1, :]))
            step("gag", inst, 16)
            inst = G(lambda e: e.dma_start(out=QB_sb[64:65, :], in_=cv[1:2, :]))
            m["crow_ld", u] = step("gag", inst, 16)
            if u == DBG_UNIT:
                dump("Q", QA_sb[:], [("gag", m["crow_ld", u]),
                                     ("dve", m["vcp", u, 7, 3])])
                dump("K", KA_sb[:], [])
                dump("VA", VA_sb, [])

        def attention(u):
            ji = 0

            def emit_av(jj, jji, tt):
                njt_ = 4 * (tt + 1)
                loj = max(0, 128 * jj - 512 * tt)
                abj = alphat_sb[:, jji % 2, :]
                blkj = jj * 130
                wait("T", "act", m["exp", u, jji])
                if jj == 0 and tt > 0:
                    wait("T", "dve", m["avev", u, tt - 1])
                    wait("T", "act", m["avevA", u, tt - 1])
                T(lambda e, blkj=blkj, abj=abj, loj=loj, jj=jj, njt_=njt_: e.matmul(
                    ps_a0[0:65, loj:512], VA_sb[:, blkj:blkj + 65],
                    abj[:, loj:512], start=(jj == 0), stop=(jj == njt_ - 1)))
                inst = T(lambda e, blkj=blkj, abj=abj, loj=loj, jj=jj, njt_=njt_: e.matmul(
                    ps_a1[0:65, loj:512], VA_sb[:, blkj + 65:blkj + 130],
                    abj[:, 512 + loj:1024],
                    start=(jj == 0), stop=(jj == njt_ - 1)))
                m["av", u, jji] = step("pe", inst)

            def tile_tail(tt, jlast):
                wait("A", "pe", m["av", u, jlast])
                if u == 0 and tt == 0:
                    wait("A", "pool", m["pool_init"])
                elif u > 0 and tt == 0:
                    wait("A", "pool", m["sumreset", u - 1])
                A(lambda e, tt=tt: e.activation(
                    SUMS[0:1, tt * 512:(tt + 1) * 512], ps_a0[64:65, 0:512],
                    AF.Identity))
                inst = A(lambda e, tt=tt: e.activation(
                    AVST[0:64, tt * 512:(tt + 1) * 512], ps_a0[0:64, 0:512],
                    AF.Identity))
                m["avevA", u, tt] = step("act", inst)
                wait("V", "pe", m["av", u, jlast])
                if u == 0 and tt == 0:
                    wait("V", "pool", m["pool_init"])
                elif u > 0 and tt == 0:
                    wait("V", "pool", m["sumreset", u - 1])
                V(lambda e, tt=tt: e.tensor_copy(SUMS[32:33, tt * 512:(tt + 1) * 512],
                                                 ps_a1[64:65, 0:512]))
                inst = V(lambda e, tt=tt: e.tensor_copy(
                    AVST[64:128, tt * 512:(tt + 1) * 512], ps_a1[0:64, 0:512]))
                m["avev", u, tt] = step("dve", inst)

            pend = None
            for t in range(8):
                njt = 4 * (t + 1)
                for j in range(njt):
                    lo = max(0, 128 * j - 512 * t)
                    psc = [ps_s0, ps_s1][ji % 2]
                    ab = alphat_sb[:, ji % 2, :]
                    if ("exp", u, ji - 2) in m:
                        wait("T", "act", m["exp", u, ji - 2])
                    wait("T", "gag", m["crow_ld", u])
                    if u == 0 and ji == 0:
                        wait("T", "pool", m["pool_init"])
                    qA = QA_sb[0:65, t * 512 + lo:(t + 1) * 512]
                    qB = QB_sb[0:65, t * 512 + lo:(t + 1) * 512]
                    kA = KA_sb[0:65, j * 128:(j + 1) * 128]
                    kB = KB_sb[0:65, j * 128:(j + 1) * 128]
                    T(lambda e, psc=psc, kA=kA, qA=qA, lo=lo: e.matmul(
                        psc[:, lo:512], kA, qA, start=True, stop=True))
                    inst = T(lambda e, psc=psc, kB=kB, qB=qB, lo=lo: e.matmul(
                        psc[:, 512 + lo:1024], kB, qB, start=True, stop=True))
                    m["sc", u, ji] = step("pe", inst)

                    if j == 1 and pend is not None:
                        emit_av(pend[0], pend[1], pend[2])

                    diag = (j >= 4 * t)
                    if diag:
                        wait("V", "pe", m["sc", u, ji])
                        V(lambda e, psc=psc, lo=lo: e.tensor_tensor(
                            psc[:, lo:lo + 128], psc[:, lo:lo + 128],
                            mtri_sb[:], ALU.add))
                        inst = V(lambda e, psc=psc, lo=lo: e.tensor_tensor(
                            psc[:, 512 + lo:512 + lo + 128],
                            psc[:, 512 + lo:512 + lo + 128],
                            mtri_sb[:], ALU.add))
                        m["mask", u, ji] = step("dve", inst)

                    wait("A", "pe", m["sc", u, ji])
                    if diag:
                        wait("A", "dve", m["mask", u, ji])
                    if ("av", u, ji - 2) in m:
                        wait("A", "pe", m["av", u, ji - 2])
                    if lo == 0:
                        inst = A(lambda e, psc=psc, ab=ab: e.activation(
                            ab[:, 0:1024], psc[:, 0:1024], AF.Exp, scale=SCALE))
                    else:
                        A(lambda e, psc=psc, ab=ab, lo=lo: e.activation(
                            ab[:, lo:512], psc[:, lo:512], AF.Exp, scale=SCALE))
                        inst = A(lambda e, psc=psc, ab=ab, lo=lo: e.activation(
                            ab[:, 512 + lo:1024], psc[:, 512 + lo:1024], AF.Exp,
                            scale=SCALE))
                    m["exp", u, ji] = step("act", inst)

                    if j == 1 and pend is not None:
                        tile_tail(pend[2], pend[1])
                        pend = None
                    if j >= 1:
                        emit_av(j - 1, ji - 1, t)
                    ji += 1
                pend = (njt - 1, ji - 1, t)
            emit_av(pend[0], pend[1], pend[2])
            tile_tail(pend[2], pend[1])
            m["avlast", u] = m["av", u, ji - 1]
            m["explast", u] = m["exp", u, ji - 1]

            if u == DBG_UNIT:
                dump("SUMS", SUMS, [("dve", m["avev", u, 7])])
                dump("AVST", AVST, [])
        def rln(u):
            for t in range(8):
                wait("A", "dve", m["avev", u, t])
                inst = A(lambda e, t=t: e.activation(RECIP[:, t * 512:(t + 1) * 512],
                                              SUMS[:, t * 512:(t + 1) * 512], AF.Ln))
                m["rlnc", u, t] = step("act", inst)
                wait("A", "act", m["rlnc", u, t])
                inst = A(lambda e, t=t: e.activation(
                    RECIP[:, t * 512:(t + 1) * 512],
                    RECIP[:, t * 512:(t + 1) * 512], AF.Exp, scale=-1.0))
                m["recipc", u, t] = step("act", inst)
            m["rln", u] = m["recipc", u, 7]
            m["recip", u] = m["recipc", u, 7]
            wait("G", "act", m["rln", u])
            m["sumreset", u] = step("pool", G(lambda e: e.memset(SUMS[:], 1.0)))
            for t in range(8):
                psb = [ps_a0, ps_a1][t % 2]
                wait("T", "act", m["recipc", u, t])
                if t == 0:
                    wait("T", "dve", m["avev", u, 7])
                if ("nrmz", u, t - 2) in m:
                    wait("T", "dve", m["nrmz", u, t - 2])
                inst = T(lambda e, psb=psb, t=t: e.matmul(
                    psb[:, 0:512], sel_sb[:], RECIP[:, t * 512:(t + 1) * 512],
                    start=True, stop=True))
                m["rbc", u, t] = step("pe", inst)
                wait("V", "pe", m["rbc", u, t])
                if u > 0 and t == 0:
                    wait("V", "pe", m["att_pe", u - 1])
                inst = V(lambda e, psb=psb, t=t: e.tensor_tensor(
                    ATTNU[:, t * 512:(t + 1) * 512], AVST[:, t * 512:(t + 1) * 512],
                    psb[:, 0:512], ALU.mult))
                m["nrmz", u, t] = step("dve", inst)
        def porj_group(u, j):
                for f in range(KD):
                    slot = [ps_a0[:, 0:512], ps_a1[:, 0:512]][f % 2]
                    if ("porj_ev", u, j, f - 2) in m:
                        wait("T", "dve", m["porj_ev", u, j, f - 2])
                    elif ("porj_ev", u, j - 1, f + 6) in m:
                        wait("T", "dve", m["porj_ev", u, j - 1, f + 6])
                    if f == 0:
                        wait("T", "dve", m["nrmz", u, 7])
                    inst = T(lambda e, j=j, f=f, slot=slot: e.matmul(
                        slot, woutc_sb[:, f * 128:(f + 1) * 128],
                        ATTNU[:, j * 512:(j + 1) * 512], start=True, stop=True))
                    m["porj", u, j, f] = step("pe", inst)
                    wait("V", "pe", m["porj", u, j, f])
                    if f == 0 and ("prs", u, j - 1) in m:
                        wait("V", "gag", m["prs", u, j - 1])
                    inst = V(lambda e, f=f, slot=slot: e.tensor_copy(
                        r3(PSTG, KD, 512)[:, f, :], slot))
                    m["porj_ev", u, j, f] = step("dve", inst)
                wait("G", "dve", m["porj_ev", u, j, KD - 1])
                inst = G(lambda e, u=u, j=j: e.dma_start(
                    out=prs_v[u][:, j, :, :], in_=r3(PSTG, KD, 512)))
                m["prs", u, j] = step("gag", inst, 16)

        def a2btail(u):
            if u == DBG_UNIT:
                dump("RECIP", RECIP, [("act", m["recip", u])])
                dump("ATTNU", ATTNU, [("dve", m["nrmz", u, 7])])
                dump("PSTG", PSTG, [("dve", m["porj_ev", u, 7, KD - 1])])
            m["att_pe", u] = m["porj", u, 7, KD - 1]

        def rs_issue(u):
            wait("G", "gag", m["prs", u, 7])
            if u > 0:
                wait("G", "cc", m["rs", u - 1])
            inst = G(lambda e, u=u: e.collective_compute(
                "ReduceScatter", ALU.add, replica_groups=[list(range(NCORE))],
                ins=[prs[u].opt()], outs=[rsout[u].opt()]))
            m["rs", u] = step("cc", inst, 1)

        def b_front(s):
            if s < 3:
                nb = (s + 1) % 2
                if ("opj_ev", s - 1, KD - 1) in m:
                    wait("G", "dve", m["opj_ev", s - 1, KD - 1])
                wait("G", "cc", m["rs", s + 1])
                inst = G(lambda e, s=s, nb=nb: e.dma_start(
                    out=r3(ATO[nb], KD, 512), in_=rsout_v[s + 1]))
                m["ato", s + 1] = step("ga" + str(nb), inst, 16)
            atv = r3(ATO[s % 2], KD, 512)
            for f in range(KD):
                wait("V", "ga" + str(s % 2), m["ato", s])
                if f == 0:
                    wait("V", "gq", m["xq", s])
                if f == 0 and s == 0:
                    wait("V", "act", m["recip", B - 1])
                inst = V(lambda e, f=f, atv=atv: e.scalar_tensor_tensor(
                    r3(XSB[s % 2], KD, 512)[:, f, :], atv[:, f, :], bout_sb[:, f:f + 1],
                    r3(XQS, KD, 512)[:, f, :], ALU.add, ALU.add))
                m["opj_ev", s, f] = step("dve", inst)
            if s == 0:
                dump("XS0", XS, [("dve", m["opj_ev", 0, KD - 1])])
                dump("ATO0", ATO[0], [])
            if s < 3:
                wait("S", "dve", m["opj_ev", s, KD - 1])
                inst = S(lambda e, s=s: e.dma_start(
                    out=r3(XQS, KD, 512), in_=xq_v[:, :, (s + 1) * 512:(s + 2) * 512]))
                m["xq", s + 1] = step("gq", inst, 16)

            wait("T", "dve", m["opj_ev", s, KD - 1])
            if s == 0:
                wait("T", "act", m["explast", B - 1])
            if s > 0:
                wait("T", "dve", m["ln1var", s - 1])
            if s > 0 and s % 2 == 0:
                wait("T", "dve", m["ln1ap", s - 1])
            for k in range(KD):
                T(lambda e, k=k: e.matmul(ps_s1[0:1, 0:512], onesf_sb[:],
                                          r3(XSB[s % 2], KD, 512)[:, k, :],
                                          start=(k == 0), stop=(k == KD - 1)))
            wait("A", "dve", m["opj_ev", s, KD - 1])
            if ("ln1mm", s - 1) in m:
                wait("A", "pe", m["ln1mm", s - 1])
            inst = A(lambda e: e.activation(XSQ[:], XSB[s % 2][:], AF.Square))
            m["ln1sq", s] = step("act", inst)
            wait("T", "act", m["ln1sq", s])
            inst = None
            for k in range(KD):
                inst = T(lambda e, k=k: e.matmul(ps_s1[0:1, 512:1024], onesb_sb[:],
                                                 r3(XSQ, KD, 512)[:, k, :],
                                                 start=(k == 0), stop=(k == KD - 1)))
            m["ln1mm", s] = step("pe", inst)
            wait("V", "pe", m["ln1mm", s])
            if s > 0:
                wait("V", "pe", m["ln1bc", s - 1])
            V(lambda e: e.tensor_scalar(statrow_sb[0:1, 0, :], ps_s1[0:1, 0:512],
                                        1.0 / D, None, ALU.mult))
            V(lambda e: e.tensor_scalar(statrow_sb[0:1, 1, :],
                                        ps_s1[0:1, 512:1024],
                                        1.0 / D, 1e-5, ALU.mult, ALU.add))
            inst = V(lambda e: e.tensor_tensor(
                statrow_sb[0:1, 2, :], statrow_sb[0:1, 0, :],
                statrow_sb[0:1, 0, :], ALU.mult))
            _sq = step("dve", inst)
            wait("V", "dve", _sq)
            inst = V(lambda e: e.tensor_tensor(statrow_sb[0:1, 1, :],
                                               statrow_sb[0:1, 1, :],
                                               statrow_sb[0:1, 2, :], ALU.subtract))
            m["ln1var", s] = step("dve", inst)
            wait("A", "dve", m["ln1var", s])
            inst = A(lambda e: e.activation(statrow_sb[0:1, 2, :],
                                            statrow_sb[0:1, 1, :], AF.Ln))
            _ln = step("act", inst)
            wait("A", "act", _ln)
            inst = A(lambda e: e.activation(statrow_sb[0:1, 3, :],
                                            statrow_sb[0:1, 2, :],
                                            AF.Exp, scale=-0.5))
            m["ln1rstd", s] = step("act", inst)

            bb = [ps_s0, ps_s1][s % 2]
            wait("T", "act", m["ln1rstd", s])
            if s >= 2:
                wait("T", "dve", m["ln1ap", s - 2])
            T(lambda e, bb=bb: e.matmul(
                bb[:, 0:512], onesrowf_sb[:], statrow_sb[0:1, 0, :],
                start=True, stop=True))
            inst = T(lambda e, bb=bb: e.matmul(
                bb[:, 512:1024], onesrowf_sb[:], statrow_sb[0:1, 3, :],
                start=True, stop=True))
            m["ln1bc", s] = step("pe", inst)

        def b_tail(s):
            bb = [ps_s0, ps_s1][s % 2]
            wait("V", "pe", m["ln1bc", s])
            for k in range(KD):
                V(lambda e, k=k, bb=bb: e.tensor_tensor(
                    r3(TMPF, KD, 512)[:, k, :],
                    r3(XSB[s % 2], KD, 512)[:, k, :],
                    bb[:, 0:512], ALU.subtract))
            inst = None
            for k in range(KD):
                inst = V(lambda e, k=k, s=s, bb=bb: e.tensor_tensor(
                    r3(X1, KD, 2048)[:, k, s * 512:(s + 1) * 512],
                    r3(TMPF, KD, 512)[:, k, :], bb[:, 512:1024], ALU.mult))
            m["ln1ap", s] = step("dve", inst)
            if s == 3:
                dump("XS", XS, [("dve", m["opj_ev", 3, KD - 1])])
                dump("ATO3", ATO[1], [])

        pjc = {uu: 0 for uu in range(B)}
        for _s in range(8):
            a1_mms(0, _s)
            if _s >= 1:
                a1_vtr(0, _s - 1)
            if _s >= 2:
                for _qb in range(4 * (_s - 2), 4 * (_s - 1)):
                    pre_block(0, _qb, _s)
        a1_vtr(0, 7)
        for _qb in range(24, 32):
            pre_block(0, _qb, 7)
        crow(0)
        for u in range(B):
            attention(u)
            rln(u)
            if u + 1 < B:
                a1_mms(u + 1, 0)
                a1_mms(u + 1, 1)
                a1_vtr(u + 1, 0)
                for _j in range(8):
                    porj_group(u, _j)
                    if _j + 2 < 8:
                        a1_mms(u + 1, _j + 2)
                        a1_vtr(u + 1, _j + 1)
                        for _qb in range(4 * _j, 4 * _j + 4):
                            pre_block(u + 1, _qb, _j + 2)
                    elif _j == 6:
                        a1_vtr(u + 1, 7)
                        for _qb in range(24, 28):
                            pre_block(u + 1, _qb, 7)
                    else:
                        for _qb in range(28, 32):
                            pre_block(u + 1, _qb, 7)
                a2btail(u)
                crow(u + 1)
                rs_issue(u)
            else:
                wait("S", "dve", m["nrmz", B - 1, 7])
                inst = S(lambda e: e.dma_start(out=r3(XQS, KD, 512),
                                               in_=xq_v[:, :, 0:512]))
                m["xq", 0] = step("gq", inst, 16)
                wait("G", "cc", m["rs", 0])
                inst = G(lambda e: e.dma_start(out=r3(ATO[0], KD, 512),
                                               in_=rsout_v[0]))
                m["ato", 0] = step("ga0", inst, 16)
                porj_group(u, 0)
                porj_group(u, 1)
                b_front(0)
                porj_group(u, 2)
                porj_group(u, 3)
                b_front(1)
                for _j in range(4, 8):
                    porj_group(u, _j)
                a2btail(u)
                rs_issue(u)
                b_tail(0)
                b_tail(1)
                b_front(2)
                b_tail(2)
                b_front(3)
                b_tail(3)

        m["x1done"] = m["ln1ap", 3]

        wait("G", "dve", m["x1done"])
        wait("G", "pe", m["ln1mm", 3])
        wait("G", "act", m["ln1sq", 3])
        step("gw", G(lambda e: e.dma_start(out=r3(WGS, KD, 768),
                                           in_=wg_v[:, :, 0:768])), 16)
        inst = G(lambda e: e.dma_start(out=r3(WUS, KD, 768), in_=wu_v[:, :, 0:768]))
        m["wgu", 0] = step("gw", inst, 16)
        inst = G(lambda e: e.dma_start(out=r3(WDS, 6, 1024), in_=wd_v[:, 0:6, :]))
        m["wd", 0] = step("gd", inst, 16)

        for hh in range(2):
            for sup in range(4):
                su = hh * 4 + sup
                for s2 in range(2):
                    tok = hh * 1024 + s2 * 512
                    for i in range(6):
                        pg = [ps_s0, ps_s1][i % 2]
                        wait("T", "gw", m["wgu", su])
                        if ("ffn_hp", su, s2, i - 2) in m:
                            wait("T", "dve", m["ffn_hp", su, s2, i - 2])
                        elif ("ffn_hp", su, s2 - 1, i + 4) in m:
                            wait("T", "dve", m["ffn_hp", su, s2 - 1, i + 4])
                        elif su > 0 and ("ffn_hp", su - 1, 1, i + 4) in m:
                            wait("T", "dve", m["ffn_hp", su - 1, 1, i + 4])
                        if su == 0 and s2 == 0 and i == 0:
                            wait("T", "dve", m["x1done"])
                        if sup == 0 and s2 == 0 and i == 0 and hh > 0:
                            wait("T", "dve", m["ln2ap", hh - 1, 1])
                        inst = None
                        for k in range(KD):
                            T(lambda e, k=k, i=i, pg=pg, tok=tok: e.matmul(
                                pg[:, 0:512],
                                WGS[:, k * 768 + i * 128: k * 768 + (i + 1) * 128],
                                r3(X1, KD, 2048)[:, k, tok:tok + 512],
                                start=(k == 0), stop=(k == KD - 1)))
                        for k in range(KD):
                            inst = T(lambda e, k=k, i=i, pg=pg, tok=tok: e.matmul(
                                pg[:, 512:1024],
                                WUS[:, k * 768 + i * 128: k * 768 + (i + 1) * 128],
                                r3(X1, KD, 2048)[:, k, tok:tok + 512],
                                start=(k == 0), stop=(k == KD - 1)))
                        m["ffn_gu", su, s2, i] = step("pe", inst)
                        wait("A", "pe", m["ffn_gu", su, s2, i])
                        if ("ffn_hp", su, s2, i - 2) in m:
                            wait("A", "dve", m["ffn_hp", su, s2, i - 2])
                        elif ("ffn_hp", su, s2 - 1, i + 4) in m:
                            wait("A", "dve", m["ffn_hp", su, s2 - 1, i + 4])
                        elif su > 0 and ("ffn_hp", su - 1, 1, i + 4) in m:
                            wait("A", "dve", m["ffn_hp", su - 1, 1, i + 4])
                        inst = A(lambda e, i=i, pg=pg: e.activation(
                            silu_sb[:, i % 2, :], pg[:, 0:512], AF.Silu, scale=FSC))
                        m["ffn_silu", su, s2, i] = step("act", inst)
                        wait("V", "act", m["ffn_silu", su, s2, i])
                        inst = V(lambda e, i=i, pg=pg: e.tensor_tensor(
                            r3(HB, 6, 512)[:, i, :], silu_sb[:, i % 2, :],
                            pg[:, 512:1024], ALU.mult))
                        m["ffn_hp", su, s2, i] = step("dve", inst)

                    wait("T", "gd", m["wd", su])
                    for f in range(KD):
                        pd = [ps_a0, ps_a1][f % 2]
                        if ("ffn_dev", su, s2, f - 2) in m:
                            wait("T", "dve", m["ffn_dev", su, s2, f - 2])
                        elif ("ffn_dev", su, s2 - 1, f + 2) in m:
                            wait("T", "dve", m["ffn_dev", su, s2 - 1, f + 2])
                        elif su > 0 and ("ffn_dev", su - 1, 1, f + 2) in m:
                            wait("T", "dve", m["ffn_dev", su - 1, 1, f + 2])
                        if f == 0:
                            wait("T", "dve", m["ffn_hp", su, s2, 5])
                        inst = None
                        for i in range(6):
                            inst = T(lambda e, i=i, f=f, pd=pd: e.matmul(
                                pd[:, 0:512],
                                r3(WDS, 6, 1024)[:, i, f * 128:(f + 1) * 128],
                                r3(HB, 6, 512)[:, i, :],
                                start=(i == 0), stop=(i == 5)))
                        m["ffn_d", su, s2, f] = step("pe", inst)
                        wait("V", "pe", m["ffn_d", su, s2, f])
                        if sup == 0 and hh == 1:
                            wait("V", "so", m["sodone", 0])
                        if sup == 0:
                            inst = V(lambda e, f=f, s2=s2, hh=hh, pd=pd:
                                     e.scalar_tensor_tensor(
                                r3(DACC, KD, 1024)[:, f, s2 * 512:(s2 + 1) * 512],
                                pd[:, 0:512], FSC * FSC,
                                r3(X1, KD, 2048)[:, f,
                                                 hh * 1024 + s2 * 512:
                                                 hh * 1024 + (s2 + 1) * 512],
                                ALU.mult, ALU.add))
                        else:
                            inst = V(lambda e, f=f, s2=s2, pd=pd:
                                     e.scalar_tensor_tensor(
                                r3(DACC, KD, 1024)[:, f, s2 * 512:(s2 + 1) * 512],
                                pd[:, 0:512], FSC * FSC,
                                r3(DACC, KD, 1024)[:, f, s2 * 512:(s2 + 1) * 512],
                                ALU.mult, ALU.add))
                        m["ffn_dev", su, s2, f] = step("dve", inst)

                if su < 7:
                    nsup = (su + 1) % 4
                    wait("G", "dve", m["ffn_hp", su, 1, 5])
                    step("gw", G(lambda e, nsup=nsup: e.dma_start(
                        out=r3(WGS, KD, 768),
                        in_=wg_v[:, :, nsup * 768:(nsup + 1) * 768])), 16)
                    inst = G(lambda e, nsup=nsup: e.dma_start(
                        out=r3(WUS, KD, 768),
                        in_=wu_v[:, :, nsup * 768:(nsup + 1) * 768]))
                    m["wgu", su + 1] = step("gw", inst, 16)
                    wait("G", "dve", m["ffn_dev", su, 1, KD - 1])
                    inst = G(lambda e, nsup=nsup: e.dma_start(
                        out=r3(WDS, 6, 1024),
                        in_=wd_v[:, nsup * 6:(nsup + 1) * 6, :]))
                    m["wd", su + 1] = step("gd", inst, 16)

            lastsu = hh * 4 + 3
            dv = r3(DACC, KD, 1024)
            if hh == 0:
                dump("DACC0", DACC, [("dve", m["ffn_dev", lastsu, 1, KD - 1])])
                dump("HB0", HB, [])
            for s2 in range(2):
                wait("T", "dve", m["ffn_dev", lastsu, 1, KD - 1])
                if ("ln2ap", hh, s2 - 1) in m:
                    wait("T", "dve", m["ln2ap", hh, s2 - 1])
                elif hh > 0 and ("ln2ap", hh - 1, 1) in m:
                    wait("T", "dve", m["ln2ap", hh - 1, 1])
                for k in range(KD):
                    T(lambda e, k=k, s2=s2: e.matmul(
                        ps_a0[0:1, 0:512], onesf_sb[:],
                        dv[:, k, s2 * 512:(s2 + 1) * 512],
                        start=(k == 0), stop=(k == KD - 1)))
                wait("A", "dve", m["ffn_dev", lastsu, s2, KD - 1])
                if ("ln2mm", hh, s2 - 1) in m:
                    wait("A", "pe", m["ln2mm", hh, s2 - 1])
                elif hh > 0 and ("ln2mm", hh - 1, 1) in m:
                    wait("A", "pe", m["ln2mm", hh - 1, 1])
                inst = A(lambda e, s2=s2: e.activation(
                    r3(XSQ2, KD, 512)[:, :, :], dv[:, :, s2 * 512:(s2 + 1) * 512],
                    AF.Square))
                m["ln2sq", hh, s2] = step("act", inst)
                wait("T", "act", m["ln2sq", hh, s2])
                inst = None
                for k in range(KD):
                    inst = T(lambda e, k=k: e.matmul(
                        ps_a1[0:1, 0:512], onesb_sb[:], r3(XSQ2, KD, 512)[:, k, :],
                        start=(k == 0), stop=(k == KD - 1)))
                m["ln2mm", hh, s2] = step("pe", inst)
                wait("V", "pe", m["ln2mm", hh, s2])
                V(lambda e: e.tensor_scalar(statrow_sb[0:1, 0, :], ps_a0[0:1, 0:512],
                                            1.0 / D, None, ALU.mult))
                V(lambda e: e.tensor_scalar(statrow_sb[0:1, 1, :], ps_a1[0:1, 0:512],
                                            1.0 / D, 1e-5, ALU.mult, ALU.add))
                inst = V(lambda e: e.tensor_tensor(statrow_sb[0:1, 2, :],
                                            statrow_sb[0:1, 0, :],
                                            statrow_sb[0:1, 0, :], ALU.mult))
                _sq2 = step("dve", inst)
                wait("V", "dve", _sq2)
                inst = V(lambda e: e.tensor_tensor(statrow_sb[0:1, 1, :],
                                                   statrow_sb[0:1, 1, :],
                                                   statrow_sb[0:1, 2, :], ALU.subtract))
                m["ln2var", hh, s2] = step("dve", inst)
                if hh == 0 and s2 == 0:
                    dump("STAT", statrow_sb[0:1, :, :].rearrange("p a b -> p (a b)"),
                         [("dve", m["ln2var", 0, 0])])
                wait("A", "dve", m["ln2var", hh, s2])
                inst = A(lambda e: e.activation(statrow_sb[0:1, 2, :],
                                                statrow_sb[0:1, 1, :], AF.Ln))
                _ln2 = step("act", inst)
                wait("A", "act", _ln2)
                inst = A(lambda e: e.activation(statrow_sb[0:1, 3, :],
                                                statrow_sb[0:1, 2, :],
                                                AF.Exp, scale=-0.5))
                m["ln2rstd", hh, s2] = step("act", inst)
                wait("T", "act", m["ln2rstd", hh, s2])
                T(lambda e: e.matmul(ps_s0[:, 0:512], onesrowf_sb[:],
                                     statrow_sb[0:1, 0, :], start=True, stop=True))
                inst = T(lambda e: e.matmul(ps_s0[:, 512:1024], onesrowf_sb[:],
                                            statrow_sb[0:1, 3, :],
                                            start=True, stop=True))
                m["ln2bc", hh, s2] = step("pe", inst)
                wait("V", "pe", m["ln2bc", hh, s2])
                for k in range(KD):
                    V(lambda e, k=k, s2=s2: e.tensor_tensor(
                        r3(TMPF2, KD, 512)[:, k, :], dv[:, k, s2 * 512:(s2 + 1) * 512],
                        ps_s0[:, 0:512], ALU.subtract))
                inst = None
                for k in range(KD):
                    inst = V(lambda e, k=k, s2=s2: e.tensor_tensor(
                        dv[:, k, s2 * 512:(s2 + 1) * 512], r3(TMPF2, KD, 512)[:, k, :],
                        ps_s0[:, 512:1024], ALU.mult))
                m["ln2ap", hh, s2] = step("dve", inst)
                wait("S", "dve", m["ln2ap", hh, s2])
                inst = S(lambda e, hh=hh, s2=s2: e.dma_start(
                    out=out_v[:, :, hh * 1024 + s2 * 512: hh * 1024 + (s2 + 1) * 512],
                    in_=dv[:, :, s2 * 512:(s2 + 1) * 512]))
                m["store", hh, s2] = step("so", inst, 16)
            m["sodone", hh] = c["so"]

        if DEBUG:
            wait("S", "so", m["sodone", 1])
            wait("S", "dve", c["dve"])
            wait("S", "act", c["act"])
            wait("S", "pe", c["pe"])
            wait("S", "dbg", c["dbg"])
            for nm, sb in [("X1", X1), ("DACC", DACC), ("HB", HB),
                           ("RS", runsum_sb[:].rearrange("p h q -> p (h q)")),
                           ("CR", crow_sb[:])]:
                step("so", S(lambda e, nm=nm, sb=sb: e.dma_start(
                    out=dbg[nm], in_=sb)), 16)
            wait("S", "so", c["so"])
        else:
            wait("S", "so", m["sodone", 1])
        return c

    with nc.Block() as block:
        @block.tensor
        def _(e):
            emit("T", e)

        @block.scalar
        def _(e):
            emit("A", e)

        @block.vector
        def _(e):
            emit("V", e)

        @block.gpsimd
        def _(e):
            emit("G", e)

        @block.sync
        def _(e):
            emit("S", e)

    stack.close()
    return nc


_NC_CACHE = None


def kernel(L0_attn_x, L0_attn_w_qkv, L0_attn_b_qkv, L0_attn_w_out, L0_attn_b_out,
           L0_ffn_w_gate, L0_ffn_w_up, L0_ffn_w_down):
    global _NC_CACHE
    x = np.asarray(L0_attn_x, dtype=np.float32)
    wqkv_h = np.asarray(L0_attn_w_qkv, dtype=np.float32)
    bqkv_h = np.asarray(L0_attn_b_qkv, dtype=np.float32)
    w_out = np.asarray(L0_attn_w_out, dtype=np.float32)
    b_out = np.asarray(L0_attn_b_out, dtype=np.float32)
    wg_h = np.asarray(L0_ffn_w_gate, dtype=np.int32)
    wu_h = np.asarray(L0_ffn_w_up, dtype=np.int32)
    wd_h = np.asarray(L0_ffn_w_down, dtype=np.int32)

    xT = np.ascontiguousarray(x.reshape(M, D).T)
    woutT = w_out.T
    wgT = np.ascontiguousarray(wg_h.T)
    wuT = np.ascontiguousarray(wu_h.T)
    wdT = np.ascontiguousarray(wd_h.T)
    boutr = np.ascontiguousarray(b_out.reshape(KD, 128).T)

    in_maps = []
    for cid in range(NCORE):
        q0 = 2 * cid * 64
        rows = np.concatenate([np.arange(q0, q0 + 128),
                               np.arange(D + q0, D + q0 + 128),
                               np.arange(2 * D + q0, 2 * D + q0 + 128)])
        wqkv_c = np.ascontiguousarray(wqkv_h[rows].T)
        bq = bqkv_h[q0:q0 + 128]
        bk = bqkv_h[D + q0:D + q0 + 128]
        bv = bqkv_h[2 * D + q0:2 * D + q0 + 128]
        bqkv_c = np.ascontiguousarray(np.stack([bq, bk, bv], axis=1))
        strip_cols = np.concatenate(
            [b * N + cid * 512 + np.arange(512) for b in range(B)])
        xq_c = np.ascontiguousarray(xT[:, strip_cols])
        woutc_c = np.ascontiguousarray(woutT[cid * 128:(cid + 1) * 128, :])
        in_maps.append({
            "xg": xT, "xq": xq_c, "wqkv": wqkv_c, "bqkv": bqkv_c,
            "woutc": woutc_c, "bout": boutr, "wg": wgT, "wu": wuT, "wd": wdT,
        })

    if _NC_CACHE is None:
        _NC_CACHE = build_nc()
    res = run_bass_kernel_spmd(_NC_CACHE, in_maps, core_ids=list(range(NCORE)))
    globals()["_LAST_RES"] = res
    outbuf = np.empty((M, D), np.float32)
    for cid in range(NCORE):
        oc = np.asarray(res.results[cid]["out"])
        for b in range(B):
            outbuf[b * N + cid * 512: b * N + (cid + 1) * 512] = \
                oc[:, b * 512:(b + 1) * 512].T
    return outbuf.reshape(B, N, D)


# version 38
# speedup vs baseline: 1.0040x; 1.0040x over previous
import contextlib
import math
import numpy as np
import ml_dtypes
import concourse.bass as bass
import concourse.mybir as mybir
from concourse.bass_utils import run_bass_kernel_spmd

F32 = mybir.dt.float32
BF16 = mybir.dt.bfloat16
I32 = mybir.dt.int32
U32 = mybir.dt.uint32
U8 = mybir.dt.uint8
AF = mybir.ActivationFunctionType
ALU = mybir.AluOpType
AXX = mybir.AxisListType.X
AX = mybir.AxisListType
ds = bass.ds

B, N, D, H, HD, HID = 4, 4096, 1024, 16, 64, 3072
M = B * N
NCORE = 8
TOWN = M // NCORE
KD = D // 128
SCALE = 0.125
CMARGIN = math.log(0.125 * 1.1)
FSC = 1.0 / 127.0
NEG = -1.0e30

O_Q = 0
O_K = 8192
O_VAUG = 16384
O_XG0 = 24832
O_XG1 = 33024
O_AVST = 41216
O_SUMS = 57600
O_RECIP = 73984
O_ATTNU = 90368
O_AT0 = 0
O_AT1 = 8192
O_WOUT = 16384
O_XQ = 32768
O_XS = 49152
O_XSQ = 65536
O_TMPF = 73984
O_X1 = 90368
O_DACC = 0
O_WG = 32768
O_WU = 45056
O_WD = 57344
O_H = 69632
O_PSTG = 98560
O_XSQ2 = 123136
O_TMPF2 = 131328
POOL = 147712

DEBUG = False
DBG_UNIT = 3


def build_nc():
    nc = bass.Bass("TRN2", target_bir_lowering=False,
                   detect_race_conditions=False)

    xg = nc.dram_tensor("xg", [D, M], F32, kind="ExternalInput").ap()
    xq = nc.dram_tensor("xq", [D, TOWN], F32, kind="ExternalInput").ap()
    wqkv = nc.dram_tensor("wqkv", [D, 384], F32, kind="ExternalInput").ap()
    bqkv = nc.dram_tensor("bqkv", [128, 3], F32, kind="ExternalInput").ap()
    woutc = nc.dram_tensor("woutc", [128, D], F32, kind="ExternalInput").ap()
    bout = nc.dram_tensor("bout", [128, KD], F32, kind="ExternalInput").ap()
    wg = nc.dram_tensor("wg", [D, HID], I32, kind="ExternalInput").ap()
    wu = nc.dram_tensor("wu", [D, HID], I32, kind="ExternalInput").ap()
    wd = nc.dram_tensor("wd", [HID, D], I32, kind="ExternalInput").ap()
    out = nc.dram_tensor("out", [D, TOWN], F32, kind="ExternalOutput").ap()
    dbg = {}
    if DEBUG:
        for nm, shape, dt in [
            ("Q", [65, 4096], BF16), ("K", [65, 4096], BF16),
            ("VA", [128, 4160], BF16), ("RS", [128, 64], F32),
            ("CR", [64, 128], BF16), ("SUMS", [33, 4096], F32),
            ("RECIP", [33, 4096], F32), ("AVST", [128, 4096], F32),
            ("ATTNU", [128, 4096], BF16), ("PSTG", [128, 4096], BF16),
            ("XS", [128, 4096], F32), ("X1", [128, 16384], BF16),
            ("DACC", [128, 8192], F32), ("HB", [128, 3072], BF16),
            ("ATO3", [128, 4096], BF16),
            ("DACC0", [128, 8192], F32), ("STAT", [1, 2048], F32),
            ("XS0", [128, 4096], F32), ("ATO0", [128, 4096], BF16),
            ("HB0", [128, 3072], BF16),
        ]:
            dbg[nm] = nc.dram_tensor("dbg_" + nm, shape, dt,
                                     kind="ExternalOutput").ap()

    prs = [nc.dram_tensor("prs%d" % u, [NCORE * D, 512], BF16, kind="Internal").ap()
           for u in range(B)]
    crow_d = nc.dram_tensor("crow_d", [64, 128], BF16, kind="Internal").ap()
    rsout = [nc.dram_tensor("rsout%d" % u, [D, 512], BF16, kind="Internal").ap()
             for u in range(B)]

    ident_c = nc.inline_tensor(np.eye(128, dtype=ml_dtypes.bfloat16), "identc").ap()
    tri = np.zeros((128, 128), np.float32)
    for p in range(128):
        tri[p, :p] = NEG
    mtri_c = nc.inline_tensor(tri, "mtric").ap()
    triu = np.zeros((128, 128), np.float32)
    for p in range(128):
        triu[p, p + 1:] = NEG
    mtriu_c = nc.inline_tensor(triu, "mtriuc").ap()
    sel = np.zeros((33, 128), np.float32)
    sel[0, 0:64] = 1.0
    sel[32, 64:128] = 1.0
    sel_c = nc.inline_tensor(sel, "selc").ap()
    ones_f_c = nc.inline_tensor(np.ones((128, 1), np.float32), "onesfc").ap()
    ones_b_c = nc.inline_tensor(np.ones((128, 1), dtype=ml_dtypes.bfloat16), "onesbc").ap()
    onesrow_f_c = nc.inline_tensor(np.ones((1, 128), np.float32), "onesrowfc").ap()
    negrow_b_c = nc.inline_tensor(np.full((1, 128), -1.0, dtype=ml_dtypes.bfloat16), "negrowbc").ap()

    xg_v = xg.rearrange("(k p) t -> p k t", p=128)
    xq_v = xq.rearrange("(k p) t -> p k t", p=128)
    wqkv_v = wqkv.rearrange("(k p) f -> p k f", p=128)
    wg_v = wg.rearrange("(k p) f -> p k f", p=128)
    wu_v = wu.rearrange("(k p) f -> p k f", p=128)
    wd_v = wd.rearrange("(i p) f -> p i f", p=128)
    prs_v = [t.rearrange("(j f p) t -> p j f t", p=128, f=KD) for t in prs]
    rsout_v = [t.rearrange("(f p) t -> p f t", p=128) for t in rsout]
    out_v = out.rearrange("(k p) t -> p k t", p=128)

    sem_names = ["pe", "act", "dve", "pool", "cc", "g", "s", "gx0", "gx1",
                 "ga0", "ga1", "gq", "gw", "gd", "gag", "so", "dbg"]

    stack = contextlib.ExitStack()
    pool = stack.enter_context(nc.sbuf_tensor("pool", [128, POOL], U8))
    wqkv_sb = stack.enter_context(nc.sbuf_tensor("wqkv_sb", [128, KD, 384], BF16))
    bqkv_sb = stack.enter_context(nc.sbuf_tensor("bqkv_sb", [128, 3], F32))
    bout_sb = stack.enter_context(nc.sbuf_tensor("bout_sb", [128, KD], F32))
    ident_sb = stack.enter_context(nc.sbuf_tensor("ident_sb", [128, 128], BF16))
    mtri_sb = stack.enter_context(nc.sbuf_tensor("mtri_sb", [128, 128], F32))
    sel_sb = stack.enter_context(nc.sbuf_tensor("sel_sb", [33, 128], F32))
    onesf_sb = stack.enter_context(nc.sbuf_tensor("onesf_sb", [128, 1], F32))
    onesb_sb = stack.enter_context(nc.sbuf_tensor("onesb_sb", [128, 1], BF16))
    onesrowf_sb = stack.enter_context(nc.sbuf_tensor("onesrowf_sb", [1, 128], F32))
    negrowb_sb = stack.enter_context(nc.sbuf_tensor("negrowb_sb", [1, 128], BF16))
    vf_sb = stack.enter_context(nc.sbuf_tensor("vf_sb", [128, 2, 512], BF16))
    QA_sb = stack.enter_context(nc.sbuf_tensor("QA_sb", [65, 4096], BF16))
    QB_sb = stack.enter_context(nc.sbuf_tensor("QB_sb", [65, 4096], BF16))
    KA_sb = stack.enter_context(nc.sbuf_tensor("KA_sb", [65, 4096], BF16))
    KB_sb = stack.enter_context(nc.sbuf_tensor("KB_sb", [65, 4096], BF16))
    scr8_sb = stack.enter_context(nc.sbuf_tensor("scr8_sb", [128, 2, 512], BF16))
    acc8_sb = stack.enter_context(nc.sbuf_tensor("acc8_sb", [128, 4], F32))
    runsum_sb = stack.enter_context(nc.sbuf_tensor("runsum_sb", [128, 2, 32], F32))
    runln_sb = stack.enter_context(nc.sbuf_tensor("runln_sb", [128, 64], BF16))
    crow_sb = stack.enter_context(nc.sbuf_tensor("crow_sb", [64, 128], BF16))
    mtriu_sb = stack.enter_context(nc.sbuf_tensor("mtriu_sb", [128, 128], F32))
    alphat_sb = stack.enter_context(nc.sbuf_tensor("alphat_sb", [128, 2, 1024], BF16))
    silu_sb = stack.enter_context(nc.sbuf_tensor("silu_sb", [128, 2, 512], BF16))
    statrow_sb = stack.enter_context(nc.sbuf_tensor("statrow_sb", [1, 4, 512], F32))
    woutc_sb = stack.enter_context(nc.sbuf_tensor("woutc_sb", [128, 1024], BF16))

    ps_s0 = stack.enter_context(nc.psum_tensor("ps_s0", [128, 1024], F32))
    ps_s1 = stack.enter_context(nc.psum_tensor("ps_s1", [128, 1024], F32))
    ps_a0 = stack.enter_context(nc.psum_tensor("ps_a0", [128, 512], F32))
    ps_a1 = stack.enter_context(nc.psum_tensor("ps_a1", [128, 512], F32))
    ps_t0 = stack.enter_context(nc.psum_tensor("ps_t0", [128, 128], BF16))
    ps_t1 = stack.enter_context(nc.psum_tensor("ps_t1", [128, 128], BF16))

    sems = {nm: stack.enter_context(nc.semaphore("sem_" + nm)) for nm in sem_names}

    def pv(off, dtype, cols, rows=128):
        nbytes = cols * (2 if dtype == BF16 else 4)
        ap = pool[0:rows, off:off + nbytes].bitcast(dtype)
        assert ap.shape == (rows, cols), (ap.shape, rows, cols)
        return ap

    VA_sb = pv(O_VAUG, BF16, 4160)
    XG = [pv(O_XG0, BF16, 4096), pv(O_XG1, BF16, 4096)]
    AVST = pv(O_AVST, F32, 4096)
    SUMS = pv(O_SUMS, F32, 4096, rows=33)
    RECIP = pv(O_RECIP, F32, 4096, rows=33)
    ATTNU = pv(O_ATTNU, BF16, 4096)
    ATO = [pv(O_AT0, BF16, 4096), pv(O_AT1, BF16, 4096)]
    XQS = pv(O_XQ, F32, 4096)
    XS = pv(O_XS, F32, 4096)
    XSQ = pv(O_XSQ, BF16, 4096)
    TMPF = pv(O_TMPF, F32, 4096)
    X1 = pv(O_X1, BF16, 16384)
    DACC = pv(O_DACC, F32, 8192)
    WGS = pv(O_WG, BF16, 6144)
    WUS = pv(O_WU, BF16, 6144)
    WDS = pv(O_WD, BF16, 6144)
    HB = pv(O_H, BF16, 3072)
    PSTG = pv(O_PSTG, BF16, 4096)
    XSQ2 = pv(O_XSQ2, BF16, 4096)
    TMPF2 = pv(O_TMPF2, F32, 4096)

    def r3(ap, k, inner):
        return ap.rearrange("p (k t) -> p k t", k=k)

    def emit(cur, eng):
        c = {nm: 0 for nm in sem_names}
        m = {}

        def on(name):
            return cur == name

        def step(sem, inst, n=1):
            c[sem] += n
            if inst is not None:
                inst.then_inc(sems[sem], n)
            return c[sem]

        def wait(target_eng, sem, val):
            if val is None or val <= 0:
                return
            if on(target_eng):
                eng.wait_ge(sems[sem], val)

        def T(fn):
            return fn(eng) if on("T") else None

        def A(fn):
            return fn(eng) if on("A") else None

        def V(fn):
            return fn(eng) if on("V") else None

        def G(fn):
            return fn(eng) if on("G") else None

        def S(fn):
            return fn(eng) if on("S") else None

        def dump(nm, src_ap, gates):
            if not DEBUG:
                return
            for semname, val in gates:
                wait("S", semname, val)
            step("dbg", S(lambda e: e.dma_start(out=dbg[nm], in_=src_ap)), 16)

        step("s", S(lambda e: e.dma_start(out=ident_sb[:], in_=ident_c)), 16)
        step("s", S(lambda e: e.dma_start(out=mtri_sb[:], in_=mtri_c)), 16)
        step("s", S(lambda e: e.dma_start(out=mtriu_sb[:], in_=mtriu_c)), 16)
        step("s", S(lambda e: e.dma_start(out=sel_sb[:], in_=sel_c)), 16)
        step("s", S(lambda e: e.dma_start(out=onesf_sb[:], in_=ones_f_c)), 16)
        step("s", S(lambda e: e.dma_start(out=onesb_sb[:], in_=ones_b_c)), 16)
        step("s", S(lambda e: e.dma_start(out=onesrowf_sb[:], in_=onesrow_f_c)), 16)
        step("s", S(lambda e: e.dma_start(out=negrowb_sb[:], in_=negrow_b_c)), 16)
        step("s", S(lambda e: e.dma_start(out=bqkv_sb[:], in_=bqkv)), 16)
        step("s", S(lambda e: e.dma_start(out=bout_sb[:], in_=bout)), 16)
        m["s_setup"] = c["s"]
        step("g", G(lambda e: e.dma_start(out=wqkv_sb[:], in_=wqkv_v)), 16)
        step("g", G(lambda e: e.dma_start(out=woutc_sb[:], in_=woutc)), 16)
        m["g_wqkv"] = c["g"]
        step("gx0", G(lambda e: e.dma_start(out=r3(XG[0], KD, 512),
                                            in_=xg_v[:, :, 0:512])), 16)
        m["xg", 0, 0] = c["gx0"]
        step("pool", G(lambda e: e.memset(VA_sb[:], 1.0)))
        step("pool", G(lambda e: e.memset(SUMS[:], 1.0)))
        step("pool", G(lambda e: e.memset(KA_sb[64:65, :], 1.0)))
        step("pool", G(lambda e: e.memset(KB_sb[64:65, :], 1.0)))
        m["pool_init"] = c["pool"]
        wait("T", "s", m["s_setup"])
        wait("A", "s", m["s_setup"])
        wait("V", "s", m["s_setup"])
        def a1_mms(u, s):
                xbuf = XG[s % 2]
                xv = r3(xbuf, KD, 512)
                ns_u, ns_s = (u, s + 1) if s < 7 else (u + 1, 0)
                if ns_u < B:
                    nb = (s + 1) % 2
                    if ("qkv_pe", u, s - 1) in m:
                        wait("G", "pe", m["qkv_pe", u, s - 1])
                    elif u > 0 and s == 0:
                        wait("G", "pe", m["qkv_pe", u - 1, 7])
                    gx = "gx" + str(nb)
                    off = ns_u * N + ns_s * 512
                    inst = G(lambda e, nb=nb, off=off: e.dma_start(
                        out=r3(XG[nb], KD, 512), in_=xg_v[:, :, off:off + 512]))
                    m["xg", ns_u, ns_s] = step(gx, inst, 16)

                wait("T", "gx" + str(s % 2), m["xg", u, s])
                wait("T", "g", m["g_wqkv"])
                if u > 0 and s == 0:
                    wait("T", "pe", m["avlast", u - 1])
                    wait("T", "act", m["explast", u - 1])
                if ("exp8prog", u) in m:
                    wait("T", "dve", m["exp8prog", u])
                if ("qkv_evac4", u, s - 1) in m:
                    wait("T", "act", m["qkv_evac4", u, s - 1])
                for k in range(KD):
                    T(lambda e, k=k: e.matmul(
                        ps_s0[:, 0:512], wqkv_sb[:, k, 0:128], xv[:, k, :],
                        start=(k == 0), stop=(k == KD - 1)))
                for k in range(KD):
                    T(lambda e, k=k: e.matmul(
                        ps_s0[:, 512:1024], wqkv_sb[:, k, 128:256], xv[:, k, :],
                        start=(k == 0), stop=(k == KD - 1)))
                if ("qkv_evac", u, s - 1) in m:
                    wait("T", "act", m["qkv_evac", u, s - 1])
                inst = None
                for k in range(KD):
                    inst = T(lambda e, k=k: e.matmul(
                        ps_s1[:, 0:512], wqkv_sb[:, k, 256:384], xv[:, k, :],
                        start=(k == 0), stop=(k == KD - 1)))
                m["qkv_pe", u, s] = step("pe", inst)

                wait("A", "pe", m["qkv_pe", u, s])
                if u > 0 and s == 0:
                    wait("A", "pe", m["avlast", u - 1])
                A(lambda e, s=s: e.activation(QA_sb[0:64, s * 512:(s + 1) * 512],
                                              ps_s0[0:64, 0:512], AF.Identity,
                                              bias=bqkv_sb[0:64, 0:1]))
                A(lambda e, s=s: e.activation(QB_sb[0:64, s * 512:(s + 1) * 512],
                                              ps_s0[64:128, 0:512], AF.Identity,
                                              bias=bqkv_sb[64:128, 0:1]))
                A(lambda e, s=s: e.activation(KA_sb[0:64, s * 512:(s + 1) * 512],
                                              ps_s0[0:64, 512:1024], AF.Identity,
                                              bias=bqkv_sb[0:64, 1:2]))
                inst = A(lambda e, s=s: e.activation(KB_sb[0:64, s * 512:(s + 1) * 512],
                                              ps_s0[64:128, 512:1024], AF.Identity,
                                              bias=bqkv_sb[64:128, 1:2]))
                m["qkv_evac4", u, s] = step("act", inst)
                if ("vtr_pe", u, s - 2) in m:
                    wait("A", "pe", m["vtr_pe", u, s - 2])
                inst = A(lambda e, s=s: e.activation(vf_sb[:, s % 2, :],
                                                     ps_s1[:, 0:512], AF.Identity,
                                                     bias=bqkv_sb[:, 2:3]))
                m["qkv_evac", u, s] = step("act", inst)

        def a1_vtr(u, s):
                wait("T", "act", m["qkv_evac", u, s])
                for tb in range(4):
                    pst = [ps_t0, ps_t1][tb % 2]
                    if ("vcp", u, s, tb - 2) in m:
                        wait("T", "dve", m["vcp", u, s, tb - 2])
                    elif ("vcp", u, s - 1, tb + 2) in m:
                        wait("T", "dve", m["vcp", u, s - 1, tb + 2])
                    elif u > 0 and s == 0:
                        wait("T", "dve", m["ccp", u - 1])
                    inst = T(lambda e, s=s, tb=tb, pst=pst: e.transpose(
                        pst[:], vf_sb[:, s % 2, tb * 128:(tb + 1) * 128], ident_sb[:]))
                    m["vtr_pe_i", u, s, tb] = step("pe", inst)
                    wait("V", "pe", m["vtr_pe_i", u, s, tb])
                    if u == 0 and s == 0 and tb == 0:
                        wait("V", "pool", m["pool_init"])
                    blk = s * 4 + tb
                    dst = VA_sb[:, blk * 130:(blk + 1) * 130].rearrange(
                        "p (g c) -> p g c", g=2)[:, :, 0:64]
                    src = pst[:].rearrange("p (g c) -> p g c", g=2)
                    inst = V(lambda e, dst=dst, src=src: e.tensor_copy(dst, src))
                    m["vcp", u, s, tb] = step("dve", inst)
                m["vtr_pe", u, s] = m["vtr_pe_i", u, s, 3]

        def pre_block(u, qb, gate_s):
                pj = pjc[u]
                nkt = (qb * 128) // 512 + 1
                scr8f = scr8_sb[:].rearrange("p h c -> p (h c)").bitcast(F32)
                for kt in range(nkt):
                    kN = min(512, (qb + 1) * 128 - kt * 512)
                    psq = [ps_s0, ps_s1][pj % 2]
                    if ("rmax8", u, pj - 2) in m:
                        wait("T", "dve", m["rmax8", u, pj - 2])
                    if kt == 0:
                        wait("T", "act", m["qkv_evac", u, gate_s])
                    if pj == 0 and u == 0:
                        wait("T", "pool", m["pool_init"])
                    qsl = slice(qb * 128, (qb + 1) * 128)
                    ksl = slice(kt * 512, kt * 512 + kN)
                    T(lambda e, psq=psq, qsl=qsl, ksl=ksl, kN=kN: e.matmul(
                        psq[:, 0:kN], QA_sb[0:64, qsl], KA_sb[0:64, ksl],
                        start=True, stop=True))
                    inst = T(lambda e, psq=psq, qsl=qsl, ksl=ksl, kN=kN: e.matmul(
                        psq[:, 512:512 + kN], QB_sb[0:64, qsl], KB_sb[0:64, ksl],
                        start=True, stop=True))
                    m["sc8", u, pj] = step("pe", inst)
                    diag = (kt == nkt - 1)
                    wait("V", "pe", m["sc8", u, pj])
                    if kt == 0 and ("rmaxq", u, qb - 1) in m:
                        wait("V", "dve", m["rmaxq", u, qb - 1])
                    if diag:
                        dcol = qb * 128 - kt * 512
                        V(lambda e, psq=psq, dcol=dcol: e.tensor_tensor(
                            psq[:, dcol:dcol + 128], psq[:, dcol:dcol + 128],
                            mtriu_sb[:], ALU.add))
                        inst = V(lambda e, psq=psq, dcol=dcol: e.tensor_tensor(
                            psq[:, 512 + dcol:512 + dcol + 128],
                            psq[:, 512 + dcol:512 + dcol + 128],
                            mtriu_sb[:], ALU.add))
                        m["msk8", u, pj] = step("dve", inst)
                        wait("V", "dve", m["msk8", u, pj])
                    V(lambda e, psq=psq, kN=kN, kt=kt: e.reduce_max(
                        scr8f[:, kt:kt + 1], psq[:, 0:kN], AXX))
                    inst = V(lambda e, psq=psq, kN=kN, kt=kt: e.reduce_max(
                        scr8f[:, 256 + kt:256 + kt + 1], psq[:, 512:512 + kN], AXX))
                    m["rmax8", u, pj] = step("dve", inst)
                    pj += 1

                wait("V", "dve", m["rmax8", u, pj - 1])
                V(lambda e, qb=qb, nkt=nkt: e.reduce_max(
                    runsum_sb[:, 0, qb:qb + 1], scr8f[:, 0:nkt], AXX))
                inst = V(lambda e, qb=qb, nkt=nkt: e.reduce_max(
                    runsum_sb[:, 1, qb:qb + 1], scr8f[:, 256:256 + nkt], AXX))
                m["rmaxq", u, qb] = step("dve", inst)

                pjc[u] = pj
                m['exp8prog', u] = m["rmax8", u, pj - 1]
        def crow(u):
            pj = pjc[u]
            wait("V", "dve", m["rmaxq", u, 31])
            inst = V(lambda e: e.tensor_scalar(runln_sb[:], runsum_sb[:].rearrange(
                "p h q -> p (h q)"), -1.0, None, ALU.mult))
            m["cneg", u] = step("dve", inst)
            wait("T", "dve", m["cneg", u])
            inst = T(lambda e: e.transpose(ps_t0[0:64, :], runln_sb[:], ident_sb[:]))
            m["ctr", u] = step("pe", inst)
            wait("V", "pe", m["ctr", u])
            inst = V(lambda e: e.tensor_copy(crow_sb[:], ps_t0[0:64, :]))
            m["ccp", u] = step("dve", inst)
            wait("G", "dve", m["ccp", u])
            inst = G(lambda e: e.dma_start(out=crow_d, in_=crow_sb[:]))
            step("gag", inst, 16)
            wait("G", "gag", c["gag"])
            cv = crow_d.rearrange("(h q) t -> h (q t)", h=2)
            inst = G(lambda e: e.dma_start(out=QA_sb[64:65, :], in_=cv[0:1, :]))
            step("gag", inst, 16)
            inst = G(lambda e: e.dma_start(out=QB_sb[64:65, :], in_=cv[1:2, :]))
            m["crow_ld", u] = step("gag", inst, 16)
            if u == DBG_UNIT:
                dump("Q", QA_sb[:], [("gag", m["crow_ld", u]),
                                     ("dve", m["vcp", u, 7, 3])])
                dump("K", KA_sb[:], [])
                dump("VA", VA_sb, [])

        def attention(u):
            ji = 0

            def emit_av(jj, jji, tt):
                njt_ = 4 * (tt + 1)
                loj = max(0, 128 * jj - 512 * tt)
                abj = alphat_sb[:, jji % 2, :]
                blkj = jj * 130
                wait("T", "act", m["exp", u, jji])
                if jj == 0 and tt > 0:
                    wait("T", "dve", m["avev", u, tt - 1])
                    wait("T", "act", m["avevA", u, tt - 1])
                T(lambda e, blkj=blkj, abj=abj, loj=loj, jj=jj, njt_=njt_: e.matmul(
                    ps_a0[0:65, loj:512], VA_sb[:, blkj:blkj + 65],
                    abj[:, loj:512], start=(jj == 0), stop=(jj == njt_ - 1)))
                inst = T(lambda e, blkj=blkj, abj=abj, loj=loj, jj=jj, njt_=njt_: e.matmul(
                    ps_a1[0:65, loj:512], VA_sb[:, blkj + 65:blkj + 130],
                    abj[:, 512 + loj:1024],
                    start=(jj == 0), stop=(jj == njt_ - 1)))
                m["av", u, jji] = step("pe", inst)

            def tile_tail(tt, jlast):
                wait("A", "pe", m["av", u, jlast])
                if u == 0 and tt == 0:
                    wait("A", "pool", m["pool_init"])
                elif u > 0 and tt == 0:
                    wait("A", "pool", m["sumreset", u - 1])
                A(lambda e, tt=tt: e.activation(
                    SUMS[0:1, tt * 512:(tt + 1) * 512], ps_a0[64:65, 0:512],
                    AF.Identity))
                inst = A(lambda e, tt=tt: e.activation(
                    AVST[0:64, tt * 512:(tt + 1) * 512], ps_a0[0:64, 0:512],
                    AF.Identity))
                m["avevA", u, tt] = step("act", inst)
                wait("V", "pe", m["av", u, jlast])
                if u == 0 and tt == 0:
                    wait("V", "pool", m["pool_init"])
                elif u > 0 and tt == 0:
                    wait("V", "pool", m["sumreset", u - 1])
                V(lambda e, tt=tt: e.tensor_copy(SUMS[32:33, tt * 512:(tt + 1) * 512],
                                                 ps_a1[64:65, 0:512]))
                inst = V(lambda e, tt=tt: e.tensor_copy(
                    AVST[64:128, tt * 512:(tt + 1) * 512], ps_a1[0:64, 0:512]))
                m["avev", u, tt] = step("dve", inst)

            pend = None
            for t in range(8):
                njt = 4 * (t + 1)
                for j in range(njt):
                    lo = max(0, 128 * j - 512 * t)
                    psc = [ps_s0, ps_s1][ji % 2]
                    ab = alphat_sb[:, ji % 2, :]
                    if ("exp", u, ji - 2) in m:
                        wait("T", "act", m["exp", u, ji - 2])
                    wait("T", "gag", m["crow_ld", u])
                    if u == 0 and ji == 0:
                        wait("T", "pool", m["pool_init"])
                    qA = QA_sb[0:65, t * 512 + lo:(t + 1) * 512]
                    qB = QB_sb[0:65, t * 512 + lo:(t + 1) * 512]
                    kA = KA_sb[0:65, j * 128:(j + 1) * 128]
                    kB = KB_sb[0:65, j * 128:(j + 1) * 128]
                    T(lambda e, psc=psc, kA=kA, qA=qA, lo=lo: e.matmul(
                        psc[:, lo:512], kA, qA, start=True, stop=True))
                    inst = T(lambda e, psc=psc, kB=kB, qB=qB, lo=lo: e.matmul(
                        psc[:, 512 + lo:1024], kB, qB, start=True, stop=True))
                    m["sc", u, ji] = step("pe", inst)

                    if j == 1 and pend is not None:
                        emit_av(pend[0], pend[1], pend[2])

                    diag = (j >= 4 * t)
                    if diag:
                        wait("V", "pe", m["sc", u, ji])
                        V(lambda e, psc=psc, lo=lo: e.tensor_tensor(
                            psc[:, lo:lo + 128], psc[:, lo:lo + 128],
                            mtri_sb[:], ALU.add))
                        inst = V(lambda e, psc=psc, lo=lo: e.tensor_tensor(
                            psc[:, 512 + lo:512 + lo + 128],
                            psc[:, 512 + lo:512 + lo + 128],
                            mtri_sb[:], ALU.add))
                        m["mask", u, ji] = step("dve", inst)

                    wait("A", "pe", m["sc", u, ji])
                    if diag:
                        wait("A", "dve", m["mask", u, ji])
                    if ("av", u, ji - 2) in m:
                        wait("A", "pe", m["av", u, ji - 2])
                    if lo == 0:
                        inst = A(lambda e, psc=psc, ab=ab: e.activation(
                            ab[:, 0:1024], psc[:, 0:1024], AF.Exp, scale=SCALE))
                    else:
                        A(lambda e, psc=psc, ab=ab, lo=lo: e.activation(
                            ab[:, lo:512], psc[:, lo:512], AF.Exp, scale=SCALE))
                        inst = A(lambda e, psc=psc, ab=ab, lo=lo: e.activation(
                            ab[:, 512 + lo:1024], psc[:, 512 + lo:1024], AF.Exp,
                            scale=SCALE))
                    m["exp", u, ji] = step("act", inst)

                    if j == 1 and pend is not None:
                        tile_tail(pend[2], pend[1])
                        pend = None
                    if j >= 1:
                        emit_av(j - 1, ji - 1, t)
                    ji += 1
                pend = (njt - 1, ji - 1, t)
            emit_av(pend[0], pend[1], pend[2])
            tile_tail(pend[2], pend[1])
            m["avlast", u] = m["av", u, ji - 1]
            m["explast", u] = m["exp", u, ji - 1]

            if u == DBG_UNIT:
                dump("SUMS", SUMS, [("dve", m["avev", u, 7])])
                dump("AVST", AVST, [])
        def rln(u):
            for t in range(8):
                wait("A", "dve", m["avev", u, t])
                inst = A(lambda e, t=t: e.activation(RECIP[:, t * 512:(t + 1) * 512],
                                              SUMS[:, t * 512:(t + 1) * 512], AF.Ln))
                m["rlnc", u, t] = step("act", inst)
                wait("A", "act", m["rlnc", u, t])
                inst = A(lambda e, t=t: e.activation(
                    RECIP[:, t * 512:(t + 1) * 512],
                    RECIP[:, t * 512:(t + 1) * 512], AF.Exp, scale=-1.0))
                m["recipc", u, t] = step("act", inst)
            m["rln", u] = m["recipc", u, 7]
            m["recip", u] = m["recipc", u, 7]
            wait("G", "act", m["rln", u])
            m["sumreset", u] = step("pool", G(lambda e: e.memset(SUMS[:], 1.0)))
            for t in range(8):
                psb = [ps_a0, ps_a1][t % 2]
                wait("T", "act", m["recipc", u, t])
                if t == 0:
                    wait("T", "dve", m["avev", u, 7])
                if ("nrmz", u, t - 2) in m:
                    wait("T", "dve", m["nrmz", u, t - 2])
                inst = T(lambda e, psb=psb, t=t: e.matmul(
                    psb[:, 0:512], sel_sb[:], RECIP[:, t * 512:(t + 1) * 512],
                    start=True, stop=True))
                m["rbc", u, t] = step("pe", inst)
                wait("V", "pe", m["rbc", u, t])
                if u > 0 and t == 0:
                    wait("V", "pe", m["att_pe", u - 1])
                inst = V(lambda e, psb=psb, t=t: e.tensor_tensor(
                    ATTNU[:, t * 512:(t + 1) * 512], AVST[:, t * 512:(t + 1) * 512],
                    psb[:, 0:512], ALU.mult))
                m["nrmz", u, t] = step("dve", inst)
        def porj_group(u, j):
                for f in range(KD):
                    slot = [ps_a0[:, 0:512], ps_a1[:, 0:512]][f % 2]
                    if ("porj_ev", u, j, f - 2) in m:
                        wait("T", "dve", m["porj_ev", u, j, f - 2])
                    elif ("porj_ev", u, j - 1, f + 6) in m:
                        wait("T", "dve", m["porj_ev", u, j - 1, f + 6])
                    if f == 0:
                        wait("T", "dve", m["nrmz", u, 7])
                    inst = T(lambda e, j=j, f=f, slot=slot: e.matmul(
                        slot, woutc_sb[:, f * 128:(f + 1) * 128],
                        ATTNU[:, j * 512:(j + 1) * 512], start=True, stop=True))
                    m["porj", u, j, f] = step("pe", inst)
                    wait("V", "pe", m["porj", u, j, f])
                    if f == 0 and ("prs", u, j - 1) in m:
                        wait("V", "gag", m["prs", u, j - 1])
                    inst = V(lambda e, f=f, slot=slot: e.tensor_copy(
                        r3(PSTG, KD, 512)[:, f, :], slot))
                    m["porj_ev", u, j, f] = step("dve", inst)
                wait("G", "dve", m["porj_ev", u, j, KD - 1])
                inst = G(lambda e, u=u, j=j: e.dma_start(
                    out=prs_v[u][:, j, :, :], in_=r3(PSTG, KD, 512)))
                m["prs", u, j] = step("gag", inst, 16)

        def a2btail(u):
            if u == DBG_UNIT:
                dump("RECIP", RECIP, [("act", m["recip", u])])
                dump("ATTNU", ATTNU, [("dve", m["nrmz", u, 7])])
                dump("PSTG", PSTG, [("dve", m["porj_ev", u, 7, KD - 1])])
            m["att_pe", u] = m["porj", u, 7, KD - 1]

        def rs_issue(u):
            wait("G", "gag", m["prs", u, 7])
            if u > 0:
                wait("G", "cc", m["rs", u - 1])
            inst = G(lambda e, u=u: e.collective_compute(
                "ReduceScatter", ALU.add, replica_groups=[list(range(NCORE))],
                ins=[prs[u].opt()], outs=[rsout[u].opt()]))
            m["rs", u] = step("cc", inst, 1)

        pjc = {uu: 0 for uu in range(B)}
        for _s in range(8):
            a1_mms(0, _s)
            if _s >= 1:
                a1_vtr(0, _s - 1)
            if _s >= 2:
                for _qb in range(4 * (_s - 2), 4 * (_s - 1)):
                    pre_block(0, _qb, _s)
        a1_vtr(0, 7)
        for _qb in range(24, 32):
            pre_block(0, _qb, 7)
        crow(0)
        for u in range(B):
            attention(u)
            rln(u)
            if u + 1 < B:
                a1_mms(u + 1, 0)
                a1_mms(u + 1, 1)
                a1_vtr(u + 1, 0)
                for _j in range(8):
                    porj_group(u, _j)
                    if _j + 2 < 8:
                        a1_mms(u + 1, _j + 2)
                        a1_vtr(u + 1, _j + 1)
                        for _qb in range(4 * _j, 4 * _j + 4):
                            pre_block(u + 1, _qb, _j + 2)
                    elif _j == 6:
                        a1_vtr(u + 1, 7)
                        for _qb in range(24, 28):
                            pre_block(u + 1, _qb, 7)
                    else:
                        for _qb in range(28, 32):
                            pre_block(u + 1, _qb, 7)
                a2btail(u)
                crow(u + 1)
            else:
                for _j in range(8):
                    porj_group(u, _j)
                a2btail(u)
            rs_issue(u)

        wait("S", "dve", m["nrmz", B - 1, 7])
        inst = S(lambda e: e.dma_start(out=r3(XQS, KD, 512), in_=xq_v[:, :, 0:512]))
        m["xq", 0] = step("gq", inst, 16)
        wait("G", "cc", m["rs", 0])
        inst = G(lambda e: e.dma_start(out=r3(ATO[0], KD, 512), in_=rsout_v[0]))
        m["ato", 0] = step("ga0", inst, 16)

        for s in range(4):
            if s < 3:
                nb = (s + 1) % 2
                if ("opj_ev", s - 1, KD - 1) in m:
                    wait("G", "dve", m["opj_ev", s - 1, KD - 1])
                wait("G", "cc", m["rs", s + 1])
                inst = G(lambda e, s=s, nb=nb: e.dma_start(
                    out=r3(ATO[nb], KD, 512), in_=rsout_v[s + 1]))
                m["ato", s + 1] = step("ga" + str(nb), inst, 16)
            atv = r3(ATO[s % 2], KD, 512)
            for f in range(KD):
                wait("V", "ga" + str(s % 2), m["ato", s])
                if f == 0:
                    wait("V", "gq", m["xq", s])
                if f == 0 and s == 0:
                    wait("V", "act", m["recip", B - 1])
                inst = V(lambda e, f=f, atv=atv: e.scalar_tensor_tensor(
                    r3(XS, KD, 512)[:, f, :], atv[:, f, :], bout_sb[:, f:f + 1],
                    r3(XQS, KD, 512)[:, f, :], ALU.add, ALU.add))
                m["opj_ev", s, f] = step("dve", inst)
            if s == 0:
                dump("XS0", XS, [("dve", m["opj_ev", 0, KD - 1])])
                dump("ATO0", ATO[0], [])
            if s < 3:
                wait("S", "dve", m["opj_ev", s, KD - 1])
                inst = S(lambda e, s=s: e.dma_start(
                    out=r3(XQS, KD, 512), in_=xq_v[:, :, (s + 1) * 512:(s + 2) * 512]))
                m["xq", s + 1] = step("gq", inst, 16)

            wait("T", "dve", m["opj_ev", s, KD - 1])
            if s == 0:
                wait("T", "dve", m["porj_ev", B - 1, 7, KD - 1])
            if s > 0:
                wait("T", "dve", m["ln1ap", s - 1])
            for k in range(KD):
                T(lambda e, k=k: e.matmul(ps_a0[0:1, 0:512], onesf_sb[:],
                                          r3(XS, KD, 512)[:, k, :],
                                          start=(k == 0), stop=(k == KD - 1)))
            wait("A", "dve", m["opj_ev", s, KD - 1])
            if ("ln1mm", s - 1) in m:
                wait("A", "pe", m["ln1mm", s - 1])
            inst = A(lambda e: e.activation(XSQ[:], XS[:], AF.Square))
            m["ln1sq", s] = step("act", inst)
            wait("T", "act", m["ln1sq", s])
            inst = None
            for k in range(KD):
                inst = T(lambda e, k=k: e.matmul(ps_a1[0:1, 0:512], onesb_sb[:],
                                                 r3(XSQ, KD, 512)[:, k, :],
                                                 start=(k == 0), stop=(k == KD - 1)))
            m["ln1mm", s] = step("pe", inst)
            wait("V", "pe", m["ln1mm", s])
            V(lambda e: e.tensor_scalar(statrow_sb[0:1, 0, :], ps_a0[0:1, 0:512],
                                        1.0 / D, None, ALU.mult))
            V(lambda e: e.tensor_scalar(statrow_sb[0:1, 1, :], ps_a1[0:1, 0:512],
                                        1.0 / D, 1e-5, ALU.mult, ALU.add))
            inst = V(lambda e: e.tensor_tensor(statrow_sb[0:1, 2, :],
                                               statrow_sb[0:1, 0, :],
                                               statrow_sb[0:1, 0, :], ALU.mult))
            _sq = step("dve", inst)
            wait("V", "dve", _sq)
            inst = V(lambda e: e.tensor_tensor(statrow_sb[0:1, 1, :],
                                               statrow_sb[0:1, 1, :],
                                               statrow_sb[0:1, 2, :], ALU.subtract))
            m["ln1var", s] = step("dve", inst)
            wait("A", "dve", m["ln1var", s])
            inst = A(lambda e: e.activation(statrow_sb[0:1, 2, :],
                                            statrow_sb[0:1, 1, :], AF.Ln))
            _ln = step("act", inst)
            wait("A", "act", _ln)
            inst = A(lambda e: e.activation(statrow_sb[0:1, 3, :],
                                            statrow_sb[0:1, 2, :], AF.Exp, scale=-0.5))
            m["ln1rstd", s] = step("act", inst)
            wait("T", "act", m["ln1rstd", s])
            if s > 0:
                wait("T", "dve", m["ln1ap", s - 1])
            if s == 0:
                wait("T", "dve", m["nrmz", B - 1, 7])
                wait("T", "dve", m["porj_ev", B - 1, 7, KD - 1])
            T(lambda e: e.matmul(ps_s0[:, 0:512], onesrowf_sb[:],
                                 statrow_sb[0:1, 0, :], start=True, stop=True))
            inst = T(lambda e: e.matmul(ps_s0[:, 512:1024], onesrowf_sb[:],
                                        statrow_sb[0:1, 3, :], start=True, stop=True))
            m["ln1bc", s] = step("pe", inst)
            wait("V", "pe", m["ln1bc", s])
            for k in range(KD):
                V(lambda e, k=k: e.tensor_tensor(r3(TMPF, KD, 512)[:, k, :],
                                                 r3(XS, KD, 512)[:, k, :],
                                                 ps_s0[:, 0:512], ALU.subtract))
            inst = None
            for k in range(KD):
                inst = V(lambda e, k=k, s=s: e.tensor_tensor(
                    r3(X1, KD, 2048)[:, k, s * 512:(s + 1) * 512],
                    r3(TMPF, KD, 512)[:, k, :], ps_s0[:, 512:1024], ALU.mult))
            m["ln1ap", s] = step("dve", inst)
            if s == 3:
                dump("XS", XS, [("dve", m["opj_ev", 3, KD - 1])])
                dump("ATO3", ATO[1], [])
        m["x1done"] = m["ln1ap", 3]

        wait("G", "dve", m["opj_ev", 3, KD - 1])
        step("gw", G(lambda e: e.dma_start(out=r3(WGS, KD, 768),
                                           in_=wg_v[:, :, 0:768])), 16)
        wait("G", "dve", m["x1done"])
        wait("G", "pe", m["ln1mm", 3])
        wait("G", "act", m["ln1sq", 3])
        inst = G(lambda e: e.dma_start(out=r3(WUS, KD, 768), in_=wu_v[:, :, 0:768]))
        m["wgu", 0] = step("gw", inst, 16)
        inst = G(lambda e: e.dma_start(out=r3(WDS, 6, 1024), in_=wd_v[:, 0:6, :]))
        m["wd", 0] = step("gd", inst, 16)

        for hh in range(2):
            for sup in range(4):
                su = hh * 4 + sup
                for s2 in range(2):
                    tok = hh * 1024 + s2 * 512
                    for i in range(6):
                        pg = [ps_s0, ps_s1][i % 2]
                        wait("T", "gw", m["wgu", su])
                        if ("ffn_hp", su, s2, i - 2) in m:
                            wait("T", "dve", m["ffn_hp", su, s2, i - 2])
                        elif ("ffn_hp", su, s2 - 1, i + 4) in m:
                            wait("T", "dve", m["ffn_hp", su, s2 - 1, i + 4])
                        elif su > 0 and ("ffn_hp", su - 1, 1, i + 4) in m:
                            wait("T", "dve", m["ffn_hp", su - 1, 1, i + 4])
                        if su == 0 and s2 == 0 and i == 0:
                            wait("T", "dve", m["x1done"])
                        if sup == 0 and s2 == 0 and i == 0 and hh > 0:
                            wait("T", "dve", m["ln2ap", hh - 1, 1])
                        inst = None
                        for k in range(KD):
                            T(lambda e, k=k, i=i, pg=pg, tok=tok: e.matmul(
                                pg[:, 0:512],
                                WGS[:, k * 768 + i * 128: k * 768 + (i + 1) * 128],
                                r3(X1, KD, 2048)[:, k, tok:tok + 512],
                                start=(k == 0), stop=(k == KD - 1)))
                        for k in range(KD):
                            inst = T(lambda e, k=k, i=i, pg=pg, tok=tok: e.matmul(
                                pg[:, 512:1024],
                                WUS[:, k * 768 + i * 128: k * 768 + (i + 1) * 128],
                                r3(X1, KD, 2048)[:, k, tok:tok + 512],
                                start=(k == 0), stop=(k == KD - 1)))
                        m["ffn_gu", su, s2, i] = step("pe", inst)
                        wait("A", "pe", m["ffn_gu", su, s2, i])
                        if ("ffn_hp", su, s2, i - 2) in m:
                            wait("A", "dve", m["ffn_hp", su, s2, i - 2])
                        elif ("ffn_hp", su, s2 - 1, i + 4) in m:
                            wait("A", "dve", m["ffn_hp", su, s2 - 1, i + 4])
                        elif su > 0 and ("ffn_hp", su - 1, 1, i + 4) in m:
                            wait("A", "dve", m["ffn_hp", su - 1, 1, i + 4])
                        inst = A(lambda e, i=i, pg=pg: e.activation(
                            silu_sb[:, i % 2, :], pg[:, 0:512], AF.Silu, scale=FSC))
                        m["ffn_silu", su, s2, i] = step("act", inst)
                        wait("V", "act", m["ffn_silu", su, s2, i])
                        inst = V(lambda e, i=i, pg=pg: e.tensor_tensor(
                            r3(HB, 6, 512)[:, i, :], silu_sb[:, i % 2, :],
                            pg[:, 512:1024], ALU.mult))
                        m["ffn_hp", su, s2, i] = step("dve", inst)

                    wait("T", "gd", m["wd", su])
                    for f in range(KD):
                        pd = [ps_a0, ps_a1][f % 2]
                        if ("ffn_dev", su, s2, f - 2) in m:
                            wait("T", "dve", m["ffn_dev", su, s2, f - 2])
                        elif ("ffn_dev", su, s2 - 1, f + 2) in m:
                            wait("T", "dve", m["ffn_dev", su, s2 - 1, f + 2])
                        elif su > 0 and ("ffn_dev", su - 1, 1, f + 2) in m:
                            wait("T", "dve", m["ffn_dev", su - 1, 1, f + 2])
                        if f == 0:
                            wait("T", "dve", m["ffn_hp", su, s2, 5])
                        inst = None
                        for i in range(6):
                            inst = T(lambda e, i=i, f=f, pd=pd: e.matmul(
                                pd[:, 0:512],
                                r3(WDS, 6, 1024)[:, i, f * 128:(f + 1) * 128],
                                r3(HB, 6, 512)[:, i, :],
                                start=(i == 0), stop=(i == 5)))
                        m["ffn_d", su, s2, f] = step("pe", inst)
                        wait("V", "pe", m["ffn_d", su, s2, f])
                        if sup == 0 and hh == 1:
                            wait("V", "so", m["sodone", 0])
                        if sup == 0:
                            inst = V(lambda e, f=f, s2=s2, hh=hh, pd=pd:
                                     e.scalar_tensor_tensor(
                                r3(DACC, KD, 1024)[:, f, s2 * 512:(s2 + 1) * 512],
                                pd[:, 0:512], FSC * FSC,
                                r3(X1, KD, 2048)[:, f,
                                                 hh * 1024 + s2 * 512:
                                                 hh * 1024 + (s2 + 1) * 512],
                                ALU.mult, ALU.add))
                        else:
                            inst = V(lambda e, f=f, s2=s2, pd=pd:
                                     e.scalar_tensor_tensor(
                                r3(DACC, KD, 1024)[:, f, s2 * 512:(s2 + 1) * 512],
                                pd[:, 0:512], FSC * FSC,
                                r3(DACC, KD, 1024)[:, f, s2 * 512:(s2 + 1) * 512],
                                ALU.mult, ALU.add))
                        m["ffn_dev", su, s2, f] = step("dve", inst)

                if su < 7:
                    nsup = (su + 1) % 4
                    wait("G", "dve", m["ffn_hp", su, 1, 5])
                    step("gw", G(lambda e, nsup=nsup: e.dma_start(
                        out=r3(WGS, KD, 768),
                        in_=wg_v[:, :, nsup * 768:(nsup + 1) * 768])), 16)
                    inst = G(lambda e, nsup=nsup: e.dma_start(
                        out=r3(WUS, KD, 768),
                        in_=wu_v[:, :, nsup * 768:(nsup + 1) * 768]))
                    m["wgu", su + 1] = step("gw", inst, 16)
                    wait("G", "dve", m["ffn_dev", su, 1, KD - 1])
                    inst = G(lambda e, nsup=nsup: e.dma_start(
                        out=r3(WDS, 6, 1024),
                        in_=wd_v[:, nsup * 6:(nsup + 1) * 6, :]))
                    m["wd", su + 1] = step("gd", inst, 16)

            lastsu = hh * 4 + 3
            dv = r3(DACC, KD, 1024)
            if hh == 0:
                dump("DACC0", DACC, [("dve", m["ffn_dev", lastsu, 1, KD - 1])])
                dump("HB0", HB, [])
            for s2 in range(2):
                wait("T", "dve", m["ffn_dev", lastsu, 1, KD - 1])
                if ("ln2ap", hh, s2 - 1) in m:
                    wait("T", "dve", m["ln2ap", hh, s2 - 1])
                elif hh > 0 and ("ln2ap", hh - 1, 1) in m:
                    wait("T", "dve", m["ln2ap", hh - 1, 1])
                for k in range(KD):
                    T(lambda e, k=k, s2=s2: e.matmul(
                        ps_a0[0:1, 0:512], onesf_sb[:],
                        dv[:, k, s2 * 512:(s2 + 1) * 512],
                        start=(k == 0), stop=(k == KD - 1)))
                wait("A", "dve", m["ffn_dev", lastsu, s2, KD - 1])
                if ("ln2mm", hh, s2 - 1) in m:
                    wait("A", "pe", m["ln2mm", hh, s2 - 1])
                elif hh > 0 and ("ln2mm", hh - 1, 1) in m:
                    wait("A", "pe", m["ln2mm", hh - 1, 1])
                inst = A(lambda e, s2=s2: e.activation(
                    r3(XSQ2, KD, 512)[:, :, :], dv[:, :, s2 * 512:(s2 + 1) * 512],
                    AF.Square))
                m["ln2sq", hh, s2] = step("act", inst)
                wait("T", "act", m["ln2sq", hh, s2])
                inst = None
                for k in range(KD):
                    inst = T(lambda e, k=k: e.matmul(
                        ps_a1[0:1, 0:512], onesb_sb[:], r3(XSQ2, KD, 512)[:, k, :],
                        start=(k == 0), stop=(k == KD - 1)))
                m["ln2mm", hh, s2] = step("pe", inst)
                wait("V", "pe", m["ln2mm", hh, s2])
                V(lambda e: e.tensor_scalar(statrow_sb[0:1, 0, :], ps_a0[0:1, 0:512],
                                            1.0 / D, None, ALU.mult))
                V(lambda e: e.tensor_scalar(statrow_sb[0:1, 1, :], ps_a1[0:1, 0:512],
                                            1.0 / D, 1e-5, ALU.mult, ALU.add))
                inst = V(lambda e: e.tensor_tensor(statrow_sb[0:1, 2, :],
                                            statrow_sb[0:1, 0, :],
                                            statrow_sb[0:1, 0, :], ALU.mult))
                _sq2 = step("dve", inst)
                wait("V", "dve", _sq2)
                inst = V(lambda e: e.tensor_tensor(statrow_sb[0:1, 1, :],
                                                   statrow_sb[0:1, 1, :],
                                                   statrow_sb[0:1, 2, :], ALU.subtract))
                m["ln2var", hh, s2] = step("dve", inst)
                if hh == 0 and s2 == 0:
                    dump("STAT", statrow_sb[0:1, :, :].rearrange("p a b -> p (a b)"),
                         [("dve", m["ln2var", 0, 0])])
                wait("A", "dve", m["ln2var", hh, s2])
                inst = A(lambda e: e.activation(statrow_sb[0:1, 2, :],
                                                statrow_sb[0:1, 1, :], AF.Ln))
                _ln2 = step("act", inst)
                wait("A", "act", _ln2)
                inst = A(lambda e: e.activation(statrow_sb[0:1, 3, :],
                                                statrow_sb[0:1, 2, :],
                                                AF.Exp, scale=-0.5))
                m["ln2rstd", hh, s2] = step("act", inst)
                wait("T", "act", m["ln2rstd", hh, s2])
                T(lambda e: e.matmul(ps_s0[:, 0:512], onesrowf_sb[:],
                                     statrow_sb[0:1, 0, :], start=True, stop=True))
                inst = T(lambda e: e.matmul(ps_s0[:, 512:1024], onesrowf_sb[:],
                                            statrow_sb[0:1, 3, :],
                                            start=True, stop=True))
                m["ln2bc", hh, s2] = step("pe", inst)
                wait("V", "pe", m["ln2bc", hh, s2])
                for k in range(KD):
                    V(lambda e, k=k, s2=s2: e.tensor_tensor(
                        r3(TMPF2, KD, 512)[:, k, :], dv[:, k, s2 * 512:(s2 + 1) * 512],
                        ps_s0[:, 0:512], ALU.subtract))
                inst = None
                for k in range(KD):
                    inst = V(lambda e, k=k, s2=s2: e.tensor_tensor(
                        dv[:, k, s2 * 512:(s2 + 1) * 512], r3(TMPF2, KD, 512)[:, k, :],
                        ps_s0[:, 512:1024], ALU.mult))
                m["ln2ap", hh, s2] = step("dve", inst)
                wait("S", "dve", m["ln2ap", hh, s2])
                inst = S(lambda e, hh=hh, s2=s2: e.dma_start(
                    out=out_v[:, :, hh * 1024 + s2 * 512: hh * 1024 + (s2 + 1) * 512],
                    in_=dv[:, :, s2 * 512:(s2 + 1) * 512]))
                m["store", hh, s2] = step("so", inst, 16)
            m["sodone", hh] = c["so"]

        if DEBUG:
            wait("S", "so", m["sodone", 1])
            wait("S", "dve", c["dve"])
            wait("S", "act", c["act"])
            wait("S", "pe", c["pe"])
            wait("S", "dbg", c["dbg"])
            for nm, sb in [("X1", X1), ("DACC", DACC), ("HB", HB),
                           ("RS", runsum_sb[:].rearrange("p h q -> p (h q)")),
                           ("CR", crow_sb[:])]:
                step("so", S(lambda e, nm=nm, sb=sb: e.dma_start(
                    out=dbg[nm], in_=sb)), 16)
            wait("S", "so", c["so"])
        else:
            wait("S", "so", m["sodone", 1])
        return c

    with nc.Block() as block:
        @block.tensor
        def _(e):
            emit("T", e)

        @block.scalar
        def _(e):
            emit("A", e)

        @block.vector
        def _(e):
            emit("V", e)

        @block.gpsimd
        def _(e):
            emit("G", e)

        @block.sync
        def _(e):
            emit("S", e)

    stack.close()
    return nc


_NC_CACHE = None


def kernel(L0_attn_x, L0_attn_w_qkv, L0_attn_b_qkv, L0_attn_w_out, L0_attn_b_out,
           L0_ffn_w_gate, L0_ffn_w_up, L0_ffn_w_down):
    global _NC_CACHE
    x = np.asarray(L0_attn_x, dtype=np.float32)
    wqkv_h = np.asarray(L0_attn_w_qkv, dtype=np.float32)
    bqkv_h = np.asarray(L0_attn_b_qkv, dtype=np.float32)
    w_out = np.asarray(L0_attn_w_out, dtype=np.float32)
    b_out = np.asarray(L0_attn_b_out, dtype=np.float32)
    wg_h = np.asarray(L0_ffn_w_gate, dtype=np.int32)
    wu_h = np.asarray(L0_ffn_w_up, dtype=np.int32)
    wd_h = np.asarray(L0_ffn_w_down, dtype=np.int32)

    xT = np.ascontiguousarray(x.reshape(M, D).T)
    woutT = w_out.T
    wgT = np.ascontiguousarray(wg_h.T)
    wuT = np.ascontiguousarray(wu_h.T)
    wdT = np.ascontiguousarray(wd_h.T)
    boutr = np.ascontiguousarray(b_out.reshape(KD, 128).T)

    in_maps = []
    for cid in range(NCORE):
        q0 = 2 * cid * 64
        rows = np.concatenate([np.arange(q0, q0 + 128),
                               np.arange(D + q0, D + q0 + 128),
                               np.arange(2 * D + q0, 2 * D + q0 + 128)])
        wqkv_c = np.ascontiguousarray(wqkv_h[rows].T)
        bq = bqkv_h[q0:q0 + 128]
        bk = bqkv_h[D + q0:D + q0 + 128]
        bv = bqkv_h[2 * D + q0:2 * D + q0 + 128]
        bqkv_c = np.ascontiguousarray(np.stack([bq, bk, bv], axis=1))
        strip_cols = np.concatenate(
            [b * N + cid * 512 + np.arange(512) for b in range(B)])
        xq_c = np.ascontiguousarray(xT[:, strip_cols])
        woutc_c = np.ascontiguousarray(woutT[cid * 128:(cid + 1) * 128, :])
        in_maps.append({
            "xg": xT, "xq": xq_c, "wqkv": wqkv_c, "bqkv": bqkv_c,
            "woutc": woutc_c, "bout": boutr, "wg": wgT, "wu": wuT, "wd": wdT,
        })

    if _NC_CACHE is None:
        _NC_CACHE = build_nc()
    res = run_bass_kernel_spmd(_NC_CACHE, in_maps, core_ids=list(range(NCORE)))
    globals()["_LAST_RES"] = res
    outbuf = np.empty((M, D), np.float32)
    for cid in range(NCORE):
        oc = np.asarray(res.results[cid]["out"])
        for b in range(B):
            outbuf[b * N + cid * 512: b * N + (cid + 1) * 512] = \
                oc[:, b * 512:(b + 1) * 512].T
    return outbuf.reshape(B, N, D)


# version 39
# speedup vs baseline: 1.0293x; 1.0252x over previous
import contextlib
import math
import numpy as np
import ml_dtypes
import concourse.bass as bass
import concourse.mybir as mybir
from concourse.bass_utils import run_bass_kernel_spmd

F32 = mybir.dt.float32
BF16 = mybir.dt.bfloat16
I32 = mybir.dt.int32
U32 = mybir.dt.uint32
U8 = mybir.dt.uint8
AF = mybir.ActivationFunctionType
ALU = mybir.AluOpType
AXX = mybir.AxisListType.X
AX = mybir.AxisListType
ds = bass.ds

B, N, D, H, HD, HID = 4, 4096, 1024, 16, 64, 3072
M = B * N
NCORE = 8
TOWN = M // NCORE
KD = D // 128
SCALE = 0.125
CMARGIN = math.log(0.125 * 1.1)
FSC = 1.0 / 127.0
NEG = -1.0e30

O_Q = 0
O_K = 8192
O_VAUG = 16384
O_XG0 = 24832
O_XG1 = 33024
O_AVST = 41216
O_SUMS = 57600
O_RECIP = 73984
O_ATTNU = 90368
O_AT0 = 0
O_AT1 = 8192
O_WOUT = 16384
O_XQ = 32768
O_XS = 49152
O_XSQ = 65536
O_TMPF = 73984
O_X1 = 90368
O_DACC = 0
O_WG = 32768
O_WU = 45056
O_WD = 57344
O_H = 69632
O_PSTG = 98560
O_XSQ2 = 123136
O_TMPF2 = 131328
POOL = 147712

DEBUG = False
DBG_UNIT = 3


def build_nc():
    nc = bass.Bass("TRN2", target_bir_lowering=False,
                   detect_race_conditions=False)

    xg = nc.dram_tensor("xg", [D, M], F32, kind="ExternalInput").ap()
    xq = nc.dram_tensor("xq", [D, TOWN], F32, kind="ExternalInput").ap()
    wqkv = nc.dram_tensor("wqkv", [D, 384], F32, kind="ExternalInput").ap()
    bqkv = nc.dram_tensor("bqkv", [128, 3], F32, kind="ExternalInput").ap()
    woutc = nc.dram_tensor("woutc", [128, D], F32, kind="ExternalInput").ap()
    bout = nc.dram_tensor("bout", [128, KD], F32, kind="ExternalInput").ap()
    wg = nc.dram_tensor("wg", [D, HID], I32, kind="ExternalInput").ap()
    wu = nc.dram_tensor("wu", [D, HID], I32, kind="ExternalInput").ap()
    wd = nc.dram_tensor("wd", [HID, D], I32, kind="ExternalInput").ap()
    out = nc.dram_tensor("out", [D, TOWN], F32, kind="ExternalOutput").ap()
    dbg = {}
    if DEBUG:
        for nm, shape, dt in [
            ("Q", [65, 4096], BF16), ("K", [65, 4096], BF16),
            ("VA", [128, 4160], BF16), ("RS", [128, 64], F32),
            ("CR", [64, 128], BF16), ("SUMS", [33, 4096], F32),
            ("RECIP", [33, 4096], F32), ("AVST", [128, 4096], F32),
            ("ATTNU", [128, 4096], BF16), ("PSTG", [128, 4096], BF16),
            ("XS", [128, 4096], F32), ("X1", [128, 16384], BF16),
            ("DACC", [128, 8192], F32), ("HB", [128, 3072], BF16),
            ("ATO3", [128, 4096], BF16),
            ("DACC0", [128, 8192], F32), ("STAT", [1, 2048], F32),
            ("XS0", [128, 4096], F32), ("ATO0", [128, 4096], BF16),
            ("HB0", [128, 3072], BF16),
        ]:
            dbg[nm] = nc.dram_tensor("dbg_" + nm, shape, dt,
                                     kind="ExternalOutput").ap()

    prs = [nc.dram_tensor("prs%d" % u, [NCORE * D, 512], BF16, kind="Internal").ap()
           for u in range(B)]
    crow_d = nc.dram_tensor("crow_d", [64, 128], BF16, kind="Internal").ap()
    rsout = [nc.dram_tensor("rsout%d" % u, [D, 512], BF16, kind="Internal").ap()
             for u in range(B)]

    ident_c = nc.inline_tensor(np.eye(128, dtype=ml_dtypes.bfloat16), "identc").ap()
    tri = np.zeros((128, 128), np.float32)
    for p in range(128):
        tri[p, :p] = NEG
    mtri_c = nc.inline_tensor(tri, "mtric").ap()
    triu = np.zeros((128, 128), np.float32)
    for p in range(128):
        triu[p, p + 1:] = NEG
    mtriu_c = nc.inline_tensor(triu, "mtriuc").ap()
    sel = np.zeros((33, 128), np.float32)
    sel[0, 0:64] = 1.0
    sel[32, 64:128] = 1.0
    sel_c = nc.inline_tensor(sel, "selc").ap()
    ones_f_c = nc.inline_tensor(np.ones((128, 1), np.float32), "onesfc").ap()
    ones_b_c = nc.inline_tensor(np.ones((128, 1), dtype=ml_dtypes.bfloat16), "onesbc").ap()
    onesrow_f_c = nc.inline_tensor(np.ones((1, 128), np.float32), "onesrowfc").ap()
    negrow_b_c = nc.inline_tensor(np.full((1, 128), -1.0, dtype=ml_dtypes.bfloat16), "negrowbc").ap()

    xg_v = xg.rearrange("(k p) t -> p k t", p=128)
    xq_v = xq.rearrange("(k p) t -> p k t", p=128)
    wqkv_v = wqkv.rearrange("(k p) f -> p k f", p=128)
    wg_v = wg.rearrange("(k p) f -> p k f", p=128)
    wu_v = wu.rearrange("(k p) f -> p k f", p=128)
    wd_v = wd.rearrange("(i p) f -> p i f", p=128)
    prs_v = [t.rearrange("(j f p) t -> p j f t", p=128, f=KD) for t in prs]
    rsout_v = [t.rearrange("(f p) t -> p f t", p=128) for t in rsout]
    out_v = out.rearrange("(k p) t -> p k t", p=128)

    sem_names = ["pe", "act", "dve", "pool", "cc", "g", "s", "gx0", "gx1",
                 "ga0", "ga1", "gq", "gw", "gd", "gag", "so", "dbg"]

    stack = contextlib.ExitStack()
    pool = stack.enter_context(nc.sbuf_tensor("pool", [128, POOL], U8))
    wqkv_sb = stack.enter_context(nc.sbuf_tensor("wqkv_sb", [128, KD, 384], BF16))
    bqkv_sb = stack.enter_context(nc.sbuf_tensor("bqkv_sb", [128, 3], F32))
    bout_sb = stack.enter_context(nc.sbuf_tensor("bout_sb", [128, KD], F32))
    ident_sb = stack.enter_context(nc.sbuf_tensor("ident_sb", [128, 128], BF16))
    mtri_sb = stack.enter_context(nc.sbuf_tensor("mtri_sb", [128, 128], F32))
    sel_sb = stack.enter_context(nc.sbuf_tensor("sel_sb", [33, 128], F32))
    onesf_sb = stack.enter_context(nc.sbuf_tensor("onesf_sb", [128, 1], F32))
    onesb_sb = stack.enter_context(nc.sbuf_tensor("onesb_sb", [128, 1], BF16))
    onesrowf_sb = stack.enter_context(nc.sbuf_tensor("onesrowf_sb", [1, 128], F32))
    negrowb_sb = stack.enter_context(nc.sbuf_tensor("negrowb_sb", [1, 128], BF16))
    vf_sb = stack.enter_context(nc.sbuf_tensor("vf_sb", [128, 2, 512], BF16))
    QA_sb = stack.enter_context(nc.sbuf_tensor("QA_sb", [65, 4096], BF16))
    QB_sb = stack.enter_context(nc.sbuf_tensor("QB_sb", [65, 4096], BF16))
    KA_sb = stack.enter_context(nc.sbuf_tensor("KA_sb", [65, 4096], BF16))
    KB_sb = stack.enter_context(nc.sbuf_tensor("KB_sb", [65, 4096], BF16))
    scr8_sb = stack.enter_context(nc.sbuf_tensor("scr8_sb", [128, 2, 512], BF16))
    acc8_sb = stack.enter_context(nc.sbuf_tensor("acc8_sb", [128, 4], F32))
    runsum_sb = stack.enter_context(nc.sbuf_tensor("runsum_sb", [128, 2, 32], F32))
    runln_sb = stack.enter_context(nc.sbuf_tensor("runln_sb", [128, 64], BF16))
    crow_sb = stack.enter_context(nc.sbuf_tensor("crow_sb", [64, 128], BF16))
    mtriu_sb = stack.enter_context(nc.sbuf_tensor("mtriu_sb", [128, 128], F32))
    alphat_sb = stack.enter_context(nc.sbuf_tensor("alphat_sb", [128, 2, 1024], BF16))
    silu_sb = stack.enter_context(nc.sbuf_tensor("silu_sb", [128, 2, 512], BF16))
    statrow_sb = stack.enter_context(nc.sbuf_tensor("statrow_sb", [1, 4, 512], F32))
    woutc_sb = stack.enter_context(nc.sbuf_tensor("woutc_sb", [128, 1024], BF16))

    ps_s0 = stack.enter_context(nc.psum_tensor("ps_s0", [128, 1024], F32))
    ps_s1 = stack.enter_context(nc.psum_tensor("ps_s1", [128, 1024], F32))
    ps_a0 = stack.enter_context(nc.psum_tensor("ps_a0", [128, 512], F32))
    ps_a1 = stack.enter_context(nc.psum_tensor("ps_a1", [128, 512], F32))
    ps_t0 = stack.enter_context(nc.psum_tensor("ps_t0", [128, 128], BF16))
    ps_t1 = stack.enter_context(nc.psum_tensor("ps_t1", [128, 128], BF16))

    sems = {nm: stack.enter_context(nc.semaphore("sem_" + nm)) for nm in sem_names}

    def pv(off, dtype, cols, rows=128):
        nbytes = cols * (2 if dtype == BF16 else 4)
        ap = pool[0:rows, off:off + nbytes].bitcast(dtype)
        assert ap.shape == (rows, cols), (ap.shape, rows, cols)
        return ap

    VA_sb = pv(O_VAUG, BF16, 4160)
    XG = [pv(O_XG0, BF16, 4096), pv(O_XG1, BF16, 4096)]
    AVST = pv(O_AVST, F32, 4096)
    SUMS = pv(O_SUMS, F32, 4096, rows=33)
    RECIP = pv(O_RECIP, F32, 4096, rows=33)
    ATTNU = pv(O_ATTNU, BF16, 4096)
    ATO = [pv(O_AT0, BF16, 4096), pv(O_AT1, BF16, 4096)]
    XQS = pv(O_XQ, F32, 4096)
    XS = pv(O_XS, F32, 4096)
    XSQ = pv(O_XSQ, BF16, 4096)
    TMPF = pv(O_TMPF, F32, 4096)
    X1 = pv(O_X1, BF16, 16384)
    DACC = pv(O_DACC, F32, 8192)
    WGS = pv(O_WG, BF16, 6144)
    WUS = pv(O_WU, BF16, 6144)
    WDS = pv(O_WD, BF16, 6144)
    HB = pv(O_H, BF16, 3072)
    PSTG = pv(O_PSTG, BF16, 4096)
    XSQ2 = pv(O_XSQ2, BF16, 4096)
    TMPF2 = pv(O_TMPF2, F32, 4096)

    def r3(ap, k, inner):
        return ap.rearrange("p (k t) -> p k t", k=k)

    def emit(cur, eng):
        c = {nm: 0 for nm in sem_names}
        m = {}

        def on(name):
            return cur == name

        def step(sem, inst, n=1):
            c[sem] += n
            if inst is not None:
                inst.then_inc(sems[sem], n)
            return c[sem]

        def wait(target_eng, sem, val):
            if val is None or val <= 0:
                return
            if on(target_eng):
                eng.wait_ge(sems[sem], val)

        def T(fn):
            return fn(eng) if on("T") else None

        def A(fn):
            return fn(eng) if on("A") else None

        def V(fn):
            return fn(eng) if on("V") else None

        def G(fn):
            return fn(eng) if on("G") else None

        def S(fn):
            return fn(eng) if on("S") else None

        def dump(nm, src_ap, gates):
            if not DEBUG:
                return
            for semname, val in gates:
                wait("S", semname, val)
            step("dbg", S(lambda e: e.dma_start(out=dbg[nm], in_=src_ap)), 16)

        step("s", S(lambda e: e.dma_start(out=ident_sb[:], in_=ident_c)), 16)
        step("s", S(lambda e: e.dma_start(out=mtri_sb[:], in_=mtri_c)), 16)
        step("s", S(lambda e: e.dma_start(out=mtriu_sb[:], in_=mtriu_c)), 16)
        step("s", S(lambda e: e.dma_start(out=sel_sb[:], in_=sel_c)), 16)
        step("s", S(lambda e: e.dma_start(out=onesf_sb[:], in_=ones_f_c)), 16)
        step("s", S(lambda e: e.dma_start(out=onesb_sb[:], in_=ones_b_c)), 16)
        step("s", S(lambda e: e.dma_start(out=onesrowf_sb[:], in_=onesrow_f_c)), 16)
        step("s", S(lambda e: e.dma_start(out=negrowb_sb[:], in_=negrow_b_c)), 16)
        step("s", S(lambda e: e.dma_start(out=bqkv_sb[:], in_=bqkv)), 16)
        step("s", S(lambda e: e.dma_start(out=bout_sb[:], in_=bout)), 16)
        m["s_setup"] = c["s"]
        step("g", G(lambda e: e.dma_start(out=wqkv_sb[:], in_=wqkv_v)), 16)
        step("g", G(lambda e: e.dma_start(out=woutc_sb[:], in_=woutc)), 16)
        m["g_wqkv"] = c["g"]
        step("gx0", G(lambda e: e.dma_start(out=r3(XG[0], KD, 512),
                                            in_=xg_v[:, :, 0:512])), 16)
        m["xg", 0, 0] = c["gx0"]
        step("pool", G(lambda e: e.memset(VA_sb[:], 1.0)))
        step("pool", G(lambda e: e.memset(SUMS[:], 1.0)))
        step("pool", G(lambda e: e.memset(KA_sb[64:65, :], 1.0)))
        step("pool", G(lambda e: e.memset(KB_sb[64:65, :], 1.0)))
        m["pool_init"] = c["pool"]
        wait("T", "s", m["s_setup"])
        wait("A", "s", m["s_setup"])
        wait("V", "s", m["s_setup"])
        def a1_mms(u, s):
                xbuf = XG[s % 2]
                xv = r3(xbuf, KD, 512)
                ns_u, ns_s = (u, s + 1) if s < 7 else (u + 1, 0)
                if ns_u < B:
                    nb = (s + 1) % 2
                    if ("qkv_pe", u, s - 1) in m:
                        wait("G", "pe", m["qkv_pe", u, s - 1])
                    elif u > 0 and s == 0:
                        wait("G", "pe", m["qkv_pe", u - 1, 7])
                    gx = "gx" + str(nb)
                    off = ns_u * N + ns_s * 512
                    inst = G(lambda e, nb=nb, off=off: e.dma_start(
                        out=r3(XG[nb], KD, 512), in_=xg_v[:, :, off:off + 512]))
                    m["xg", ns_u, ns_s] = step(gx, inst, 16)

                wait("T", "gx" + str(s % 2), m["xg", u, s])
                wait("T", "g", m["g_wqkv"])
                if u > 0 and s == 0:
                    wait("T", "pe", m["avlast", u - 1])
                    wait("T", "act", m["explast", u - 1])
                if ("exp8prog", u) in m:
                    wait("T", "dve", m["exp8prog", u])
                if ("qkv_evac4", u, s - 1) in m:
                    wait("T", "act", m["qkv_evac4", u, s - 1])
                for k in range(KD):
                    T(lambda e, k=k: e.matmul(
                        ps_s0[:, 0:512], wqkv_sb[:, k, 0:128], xv[:, k, :],
                        start=(k == 0), stop=(k == KD - 1)))
                for k in range(KD):
                    T(lambda e, k=k: e.matmul(
                        ps_s0[:, 512:1024], wqkv_sb[:, k, 128:256], xv[:, k, :],
                        start=(k == 0), stop=(k == KD - 1)))
                if ("qkv_evac", u, s - 1) in m:
                    wait("T", "act", m["qkv_evac", u, s - 1])
                inst = None
                for k in range(KD):
                    inst = T(lambda e, k=k: e.matmul(
                        ps_s1[:, 0:512], wqkv_sb[:, k, 256:384], xv[:, k, :],
                        start=(k == 0), stop=(k == KD - 1)))
                m["qkv_pe", u, s] = step("pe", inst)

                wait("A", "pe", m["qkv_pe", u, s])
                if u > 0 and s == 0:
                    wait("A", "pe", m["avlast", u - 1])
                A(lambda e, s=s: e.activation(QA_sb[0:64, s * 512:(s + 1) * 512],
                                              ps_s0[0:64, 0:512], AF.Identity,
                                              bias=bqkv_sb[0:64, 0:1]))
                A(lambda e, s=s: e.activation(QB_sb[0:64, s * 512:(s + 1) * 512],
                                              ps_s0[64:128, 0:512], AF.Identity,
                                              bias=bqkv_sb[64:128, 0:1]))
                A(lambda e, s=s: e.activation(KA_sb[0:64, s * 512:(s + 1) * 512],
                                              ps_s0[0:64, 512:1024], AF.Identity,
                                              bias=bqkv_sb[0:64, 1:2]))
                inst = A(lambda e, s=s: e.activation(KB_sb[0:64, s * 512:(s + 1) * 512],
                                              ps_s0[64:128, 512:1024], AF.Identity,
                                              bias=bqkv_sb[64:128, 1:2]))
                m["qkv_evac4", u, s] = step("act", inst)
                if ("vtr_pe", u, s - 2) in m:
                    wait("A", "pe", m["vtr_pe", u, s - 2])
                inst = A(lambda e, s=s: e.activation(vf_sb[:, s % 2, :],
                                                     ps_s1[:, 0:512], AF.Identity,
                                                     bias=bqkv_sb[:, 2:3]))
                m["qkv_evac", u, s] = step("act", inst)

        def a1_vtr(u, s):
                wait("T", "act", m["qkv_evac", u, s])
                for tb in range(4):
                    pst = [ps_t0, ps_t1][tb % 2]
                    if ("vcp", u, s, tb - 2) in m:
                        wait("T", "dve", m["vcp", u, s, tb - 2])
                    elif ("vcp", u, s - 1, tb + 2) in m:
                        wait("T", "dve", m["vcp", u, s - 1, tb + 2])
                    elif u > 0 and s == 0:
                        wait("T", "dve", m["ccp", u - 1])
                    inst = T(lambda e, s=s, tb=tb, pst=pst: e.transpose(
                        pst[:], vf_sb[:, s % 2, tb * 128:(tb + 1) * 128], ident_sb[:]))
                    m["vtr_pe_i", u, s, tb] = step("pe", inst)
                    wait("V", "pe", m["vtr_pe_i", u, s, tb])
                    if u == 0 and s == 0 and tb == 0:
                        wait("V", "pool", m["pool_init"])
                    blk = s * 4 + tb
                    dst = VA_sb[:, blk * 130:(blk + 1) * 130].rearrange(
                        "p (g c) -> p g c", g=2)[:, :, 0:64]
                    src = pst[:].rearrange("p (g c) -> p g c", g=2)
                    inst = V(lambda e, dst=dst, src=src: e.tensor_copy(dst, src))
                    m["vcp", u, s, tb] = step("dve", inst)
                m["vtr_pe", u, s] = m["vtr_pe_i", u, s, 3]

        def pre_block(u, qb, gate_s):
                pj = pjc[u]
                nkt = (qb * 128) // 512 + 1
                scr8f = scr8_sb[:].rearrange("p h c -> p (h c)").bitcast(F32)
                for kt in range(nkt):
                    kN = min(512, (qb + 1) * 128 - kt * 512)
                    psq = [ps_s0, ps_s1][pj % 2]
                    if ("rmax8", u, pj - 2) in m:
                        wait("T", "dve", m["rmax8", u, pj - 2])
                    if kt == 0:
                        wait("T", "act", m["qkv_evac", u, gate_s])
                    if pj == 0 and u == 0:
                        wait("T", "pool", m["pool_init"])
                    qsl = slice(qb * 128, (qb + 1) * 128)
                    ksl = slice(kt * 512, kt * 512 + kN)
                    T(lambda e, psq=psq, qsl=qsl, ksl=ksl, kN=kN: e.matmul(
                        psq[:, 0:kN], QA_sb[0:64, qsl], KA_sb[0:64, ksl],
                        start=True, stop=True))
                    inst = T(lambda e, psq=psq, qsl=qsl, ksl=ksl, kN=kN: e.matmul(
                        psq[:, 512:512 + kN], QB_sb[0:64, qsl], KB_sb[0:64, ksl],
                        start=True, stop=True))
                    m["sc8", u, pj] = step("pe", inst)
                    diag = (kt == nkt - 1)
                    wait("V", "pe", m["sc8", u, pj])
                    if kt == 0 and ("rmaxq", u, qb - 1) in m:
                        wait("V", "dve", m["rmaxq", u, qb - 1])
                    if diag:
                        dcol = qb * 128 - kt * 512
                        V(lambda e, psq=psq, dcol=dcol: e.tensor_tensor(
                            psq[:, dcol:dcol + 128], psq[:, dcol:dcol + 128],
                            mtriu_sb[:], ALU.add))
                        inst = V(lambda e, psq=psq, dcol=dcol: e.tensor_tensor(
                            psq[:, 512 + dcol:512 + dcol + 128],
                            psq[:, 512 + dcol:512 + dcol + 128],
                            mtriu_sb[:], ALU.add))
                        m["msk8", u, pj] = step("dve", inst)
                        wait("V", "dve", m["msk8", u, pj])
                    V(lambda e, psq=psq, kN=kN, kt=kt: e.reduce_max(
                        scr8f[:, kt:kt + 1], psq[:, 0:kN], AXX))
                    inst = V(lambda e, psq=psq, kN=kN, kt=kt: e.reduce_max(
                        scr8f[:, 256 + kt:256 + kt + 1], psq[:, 512:512 + kN], AXX))
                    m["rmax8", u, pj] = step("dve", inst)
                    pj += 1

                wait("V", "dve", m["rmax8", u, pj - 1])
                V(lambda e, qb=qb, nkt=nkt: e.reduce_max(
                    runsum_sb[:, 0, qb:qb + 1], scr8f[:, 0:nkt], AXX))
                inst = V(lambda e, qb=qb, nkt=nkt: e.reduce_max(
                    runsum_sb[:, 1, qb:qb + 1], scr8f[:, 256:256 + nkt], AXX))
                m["rmaxq", u, qb] = step("dve", inst)

                pjc[u] = pj
                m['exp8prog', u] = m["rmax8", u, pj - 1]
        def crow(u):
            pj = pjc[u]
            wait("V", "dve", m["rmaxq", u, 31])
            inst = V(lambda e: e.tensor_scalar(runln_sb[:], runsum_sb[:].rearrange(
                "p h q -> p (h q)"), -1.0, None, ALU.mult))
            m["cneg", u] = step("dve", inst)
            wait("T", "dve", m["cneg", u])
            inst = T(lambda e: e.transpose(ps_t0[0:64, :], runln_sb[:], ident_sb[:]))
            m["ctr", u] = step("pe", inst)
            wait("V", "pe", m["ctr", u])
            inst = V(lambda e: e.tensor_copy(crow_sb[:], ps_t0[0:64, :]))
            m["ccp", u] = step("dve", inst)
            wait("G", "dve", m["ccp", u])
            inst = G(lambda e: e.dma_start(out=crow_d, in_=crow_sb[:]))
            step("gag", inst, 16)
            wait("G", "gag", c["gag"])
            cv = crow_d.rearrange("(h q) t -> h (q t)", h=2)
            inst = G(lambda e: e.dma_start(out=QA_sb[64:65, :], in_=cv[0:1, :]))
            step("gag", inst, 16)
            inst = G(lambda e: e.dma_start(out=QB_sb[64:65, :], in_=cv[1:2, :]))
            m["crow_ld", u] = step("gag", inst, 16)
            if u == DBG_UNIT:
                dump("Q", QA_sb[:], [("gag", m["crow_ld", u]),
                                     ("dve", m["vcp", u, 7, 3])])
                dump("K", KA_sb[:], [])
                dump("VA", VA_sb, [])

        def attention(u):
            ji = 0

            def emit_av(jj, jji, tt):
                njt_ = 4 * (tt + 1)
                loj = max(0, 128 * jj - 512 * tt)
                abj = alphat_sb[:, jji % 2, :]
                blkj = jj * 130
                wait("T", "act", m["exp", u, jji])
                if jj == 0 and tt > 0:
                    wait("T", "dve", m["avev", u, tt - 1])
                    wait("T", "act", m["avevA", u, tt - 1])
                T(lambda e, blkj=blkj, abj=abj, loj=loj, jj=jj, njt_=njt_: e.matmul(
                    ps_a0[0:65, loj:512], VA_sb[:, blkj:blkj + 65],
                    abj[:, loj:512], start=(jj == 0), stop=(jj == njt_ - 1)))
                inst = T(lambda e, blkj=blkj, abj=abj, loj=loj, jj=jj, njt_=njt_: e.matmul(
                    ps_a1[0:65, loj:512], VA_sb[:, blkj + 65:blkj + 130],
                    abj[:, 512 + loj:1024],
                    start=(jj == 0), stop=(jj == njt_ - 1)))
                m["av", u, jji] = step("pe", inst)

            def tile_tail(tt, jlast):
                wait("A", "pe", m["av", u, jlast])
                if u == 0 and tt == 0:
                    wait("A", "pool", m["pool_init"])
                elif u > 0 and tt == 0:
                    wait("A", "pool", m["sumreset", u - 1])
                A(lambda e, tt=tt: e.activation(
                    SUMS[0:1, tt * 512:(tt + 1) * 512], ps_a0[64:65, 0:512],
                    AF.Identity))
                inst = A(lambda e, tt=tt: e.activation(
                    AVST[0:64, tt * 512:(tt + 1) * 512], ps_a0[0:64, 0:512],
                    AF.Identity))
                m["avevA", u, tt] = step("act", inst)
                wait("V", "pe", m["av", u, jlast])
                if u == 0 and tt == 0:
                    wait("V", "pool", m["pool_init"])
                elif u > 0 and tt == 0:
                    wait("V", "pool", m["sumreset", u - 1])
                V(lambda e, tt=tt: e.tensor_copy(SUMS[32:33, tt * 512:(tt + 1) * 512],
                                                 ps_a1[64:65, 0:512]))
                inst = V(lambda e, tt=tt: e.tensor_copy(
                    AVST[64:128, tt * 512:(tt + 1) * 512], ps_a1[0:64, 0:512]))
                m["avev", u, tt] = step("dve", inst)

            pend = None
            for t in range(8):
                njt = 4 * (t + 1)
                for j in range(njt):
                    lo = max(0, 128 * j - 512 * t)
                    psc = [ps_s0, ps_s1][ji % 2]
                    ab = alphat_sb[:, ji % 2, :]
                    if ("exp", u, ji - 2) in m:
                        wait("T", "act", m["exp", u, ji - 2])
                    wait("T", "gag", m["crow_ld", u])
                    if u == 0 and ji == 0:
                        wait("T", "pool", m["pool_init"])
                    qA = QA_sb[0:65, t * 512 + lo:(t + 1) * 512]
                    qB = QB_sb[0:65, t * 512 + lo:(t + 1) * 512]
                    kA = KA_sb[0:65, j * 128:(j + 1) * 128]
                    kB = KB_sb[0:65, j * 128:(j + 1) * 128]
                    T(lambda e, psc=psc, kA=kA, qA=qA, lo=lo: e.matmul(
                        psc[:, lo:512], kA, qA, start=True, stop=True))
                    inst = T(lambda e, psc=psc, kB=kB, qB=qB, lo=lo: e.matmul(
                        psc[:, 512 + lo:1024], kB, qB, start=True, stop=True))
                    m["sc", u, ji] = step("pe", inst)

                    if j == 1 and pend is not None:
                        emit_av(pend[0], pend[1], pend[2])

                    diag = (j >= 4 * t)
                    if diag:
                        wait("V", "pe", m["sc", u, ji])
                        V(lambda e, psc=psc, lo=lo: e.tensor_tensor(
                            psc[:, lo:lo + 128], psc[:, lo:lo + 128],
                            mtri_sb[:], ALU.add))
                        inst = V(lambda e, psc=psc, lo=lo: e.tensor_tensor(
                            psc[:, 512 + lo:512 + lo + 128],
                            psc[:, 512 + lo:512 + lo + 128],
                            mtri_sb[:], ALU.add))
                        m["mask", u, ji] = step("dve", inst)

                    wait("A", "pe", m["sc", u, ji])
                    if diag:
                        wait("A", "dve", m["mask", u, ji])
                    if ("av", u, ji - 2) in m:
                        wait("A", "pe", m["av", u, ji - 2])
                    if lo == 0:
                        inst = A(lambda e, psc=psc, ab=ab: e.activation(
                            ab[:, 0:1024], psc[:, 0:1024], AF.Exp, scale=SCALE))
                    else:
                        A(lambda e, psc=psc, ab=ab, lo=lo: e.activation(
                            ab[:, lo:512], psc[:, lo:512], AF.Exp, scale=SCALE))
                        inst = A(lambda e, psc=psc, ab=ab, lo=lo: e.activation(
                            ab[:, 512 + lo:1024], psc[:, 512 + lo:1024], AF.Exp,
                            scale=SCALE))
                    m["exp", u, ji] = step("act", inst)

                    if j == 1 and pend is not None:
                        tile_tail(pend[2], pend[1])
                        pend = None
                    if j >= 1:
                        emit_av(j - 1, ji - 1, t)
                    ji += 1
                pend = (njt - 1, ji - 1, t)
            emit_av(pend[0], pend[1], pend[2])
            tile_tail(pend[2], pend[1])
            m["avlast", u] = m["av", u, ji - 1]
            m["explast", u] = m["exp", u, ji - 1]

            if u == DBG_UNIT:
                dump("SUMS", SUMS, [("dve", m["avev", u, 7])])
                dump("AVST", AVST, [])
        def rln(u):
            for t in range(8):
                wait("A", "dve", m["avev", u, t])
                inst = A(lambda e, t=t: e.activation(RECIP[:, t * 512:(t + 1) * 512],
                                              SUMS[:, t * 512:(t + 1) * 512], AF.Ln))
                m["rlnc", u, t] = step("act", inst)
                wait("A", "act", m["rlnc", u, t])
                inst = A(lambda e, t=t: e.activation(
                    RECIP[:, t * 512:(t + 1) * 512],
                    RECIP[:, t * 512:(t + 1) * 512], AF.Exp, scale=-1.0))
                m["recipc", u, t] = step("act", inst)
            m["rln", u] = m["recipc", u, 7]
            m["recip", u] = m["recipc", u, 7]
            wait("G", "act", m["rln", u])
            m["sumreset", u] = step("pool", G(lambda e: e.memset(SUMS[:], 1.0)))
            for t in range(8):
                psb = [ps_a0, ps_a1][t % 2]
                wait("T", "act", m["recipc", u, t])
                if t == 0:
                    wait("T", "dve", m["avev", u, 7])
                if ("nrmz", u, t - 2) in m:
                    wait("T", "dve", m["nrmz", u, t - 2])
                inst = T(lambda e, psb=psb, t=t: e.matmul(
                    psb[:, 0:512], sel_sb[:], RECIP[:, t * 512:(t + 1) * 512],
                    start=True, stop=True))
                m["rbc", u, t] = step("pe", inst)
                wait("V", "pe", m["rbc", u, t])
                if u > 0 and t == 0:
                    wait("V", "pe", m["att_pe", u - 1])
                inst = V(lambda e, psb=psb, t=t: e.tensor_tensor(
                    ATTNU[:, t * 512:(t + 1) * 512], AVST[:, t * 512:(t + 1) * 512],
                    psb[:, 0:512], ALU.mult))
                m["nrmz", u, t] = step("dve", inst)
        def porj_group(u, j):
                for f in range(KD):
                    slot = [ps_a0[:, 0:512], ps_a1[:, 0:512]][f % 2]
                    if ("porj_ev", u, j, f - 2) in m:
                        wait("T", "dve", m["porj_ev", u, j, f - 2])
                    elif ("porj_ev", u, j - 1, f + 6) in m:
                        wait("T", "dve", m["porj_ev", u, j - 1, f + 6])
                    if f == 0:
                        wait("T", "dve", m["nrmz", u, 7])
                    inst = T(lambda e, j=j, f=f, slot=slot: e.matmul(
                        slot, woutc_sb[:, f * 128:(f + 1) * 128],
                        ATTNU[:, j * 512:(j + 1) * 512], start=True, stop=True))
                    m["porj", u, j, f] = step("pe", inst)
                    wait("V", "pe", m["porj", u, j, f])
                    if f == 0 and ("prs", u, j - 1) in m:
                        wait("V", "gag", m["prs", u, j - 1])
                    inst = V(lambda e, f=f, slot=slot: e.tensor_copy(
                        r3(PSTG, KD, 512)[:, f, :], slot))
                    m["porj_ev", u, j, f] = step("dve", inst)
                wait("G", "dve", m["porj_ev", u, j, KD - 1])
                inst = G(lambda e, u=u, j=j: e.dma_start(
                    out=prs_v[u][:, j, :, :], in_=r3(PSTG, KD, 512)))
                m["prs", u, j] = step("gag", inst, 16)

        def a2btail(u):
            if u == DBG_UNIT:
                dump("RECIP", RECIP, [("act", m["recip", u])])
                dump("ATTNU", ATTNU, [("dve", m["nrmz", u, 7])])
                dump("PSTG", PSTG, [("dve", m["porj_ev", u, 7, KD - 1])])
            m["att_pe", u] = m["porj", u, 7, KD - 1]

        def rs_issue(u):
            wait("G", "gag", m["prs", u, 7])
            if u > 0:
                wait("G", "cc", m["rs", u - 1])
            inst = G(lambda e, u=u: e.collective_compute(
                "ReduceScatter", ALU.add, replica_groups=[list(range(NCORE))],
                ins=[prs[u].opt()], outs=[rsout[u].opt()]))
            m["rs", u] = step("cc", inst, 1)

        pjc = {uu: 0 for uu in range(B)}
        for _s in range(8):
            a1_mms(0, _s)
            if _s >= 1:
                a1_vtr(0, _s - 1)
            if _s >= 2:
                for _qb in range(4 * (_s - 2), 4 * (_s - 1)):
                    pre_block(0, _qb, _s)
        a1_vtr(0, 7)
        for _qb in range(24, 32):
            pre_block(0, _qb, 7)
        crow(0)
        for u in range(B):
            attention(u)
            rln(u)
            if u + 1 < B:
                a1_mms(u + 1, 0)
                a1_mms(u + 1, 1)
                a1_vtr(u + 1, 0)
                for _j in range(8):
                    porj_group(u, _j)
                    if _j + 2 < 8:
                        a1_mms(u + 1, _j + 2)
                        a1_vtr(u + 1, _j + 1)
                        for _qb in range(4 * _j, 4 * _j + 4):
                            pre_block(u + 1, _qb, _j + 2)
                    elif _j == 6:
                        a1_vtr(u + 1, 7)
                        for _qb in range(24, 28):
                            pre_block(u + 1, _qb, 7)
                    else:
                        for _qb in range(28, 32):
                            pre_block(u + 1, _qb, 7)
                a2btail(u)
                crow(u + 1)
            else:
                for _j in range(8):
                    porj_group(u, _j)
                a2btail(u)
            rs_issue(u)

        wait("S", "dve", m["nrmz", B - 1, 7])
        inst = S(lambda e: e.dma_start(out=r3(XQS, KD, 512), in_=xq_v[:, :, 0:512]))
        m["xq", 0] = step("gq", inst, 16)
        wait("G", "cc", m["rs", 0])
        inst = G(lambda e: e.dma_start(out=r3(ATO[0], KD, 512), in_=rsout_v[0]))
        m["ato", 0] = step("ga0", inst, 16)

        for s in range(4):
            if s < 3:
                nb = (s + 1) % 2
                if ("opj_ev", s - 1, KD - 1) in m:
                    wait("G", "dve", m["opj_ev", s - 1, KD - 1])
                wait("G", "cc", m["rs", s + 1])
                inst = G(lambda e, s=s, nb=nb: e.dma_start(
                    out=r3(ATO[nb], KD, 512), in_=rsout_v[s + 1]))
                m["ato", s + 1] = step("ga" + str(nb), inst, 16)
            atv = r3(ATO[s % 2], KD, 512)
            for f in range(KD):
                wait("V", "ga" + str(s % 2), m["ato", s])
                if f == 0:
                    wait("V", "gq", m["xq", s])
                if f == 0 and s == 0:
                    wait("V", "act", m["recip", B - 1])
                inst = V(lambda e, f=f, atv=atv: e.scalar_tensor_tensor(
                    r3(XS, KD, 512)[:, f, :], atv[:, f, :], bout_sb[:, f:f + 1],
                    r3(XQS, KD, 512)[:, f, :], ALU.add, ALU.add))
                m["opj_ev", s, f] = step("dve", inst)
            if s == 0:
                dump("XS0", XS, [("dve", m["opj_ev", 0, KD - 1])])
                dump("ATO0", ATO[0], [])
            if s < 3:
                wait("S", "dve", m["opj_ev", s, KD - 1])
                inst = S(lambda e, s=s: e.dma_start(
                    out=r3(XQS, KD, 512), in_=xq_v[:, :, (s + 1) * 512:(s + 2) * 512]))
                m["xq", s + 1] = step("gq", inst, 16)

            wait("T", "dve", m["opj_ev", s, KD - 1])
            if s == 0:
                wait("T", "dve", m["porj_ev", B - 1, 7, KD - 1])
            if s > 0:
                wait("T", "dve", m["ln1ap", s - 1])
            for k in range(KD):
                T(lambda e, k=k: e.matmul(ps_a0[0:1, 0:512], onesf_sb[:],
                                          r3(XS, KD, 512)[:, k, :],
                                          start=(k == 0), stop=(k == KD - 1)))
            wait("A", "dve", m["opj_ev", s, KD - 1])
            if ("ln1mm", s - 1) in m:
                wait("A", "pe", m["ln1mm", s - 1])
            inst = A(lambda e: e.activation(XSQ[:], XS[:], AF.Square))
            m["ln1sq", s] = step("act", inst)
            wait("T", "act", m["ln1sq", s])
            inst = None
            for k in range(KD):
                inst = T(lambda e, k=k: e.matmul(ps_a1[0:1, 0:512], onesb_sb[:],
                                                 r3(XSQ, KD, 512)[:, k, :],
                                                 start=(k == 0), stop=(k == KD - 1)))
            m["ln1mm", s] = step("pe", inst)
            wait("V", "pe", m["ln1mm", s])
            V(lambda e: e.tensor_scalar(statrow_sb[0:1, 0, :], ps_a0[0:1, 0:512],
                                        1.0 / D, None, ALU.mult))
            V(lambda e: e.tensor_scalar(statrow_sb[0:1, 1, :], ps_a1[0:1, 0:512],
                                        1.0 / D, 1e-5, ALU.mult, ALU.add))
            inst = V(lambda e: e.tensor_tensor(statrow_sb[0:1, 2, :],
                                               statrow_sb[0:1, 0, :],
                                               statrow_sb[0:1, 0, :], ALU.mult))
            _sq = step("dve", inst)
            wait("V", "dve", _sq)
            inst = V(lambda e: e.tensor_tensor(statrow_sb[0:1, 1, :],
                                               statrow_sb[0:1, 1, :],
                                               statrow_sb[0:1, 2, :], ALU.subtract))
            m["ln1var", s] = step("dve", inst)
            wait("A", "dve", m["ln1var", s])
            inst = A(lambda e: e.activation(statrow_sb[0:1, 2, :],
                                            statrow_sb[0:1, 1, :], AF.Ln))
            _ln = step("act", inst)
            wait("A", "act", _ln)
            inst = A(lambda e: e.activation(statrow_sb[0:1, 3, :],
                                            statrow_sb[0:1, 2, :], AF.Exp, scale=-0.5))
            m["ln1rstd", s] = step("act", inst)
            wait("T", "act", m["ln1rstd", s])
            if s > 0:
                wait("T", "dve", m["ln1ap", s - 1])
            if s == 0:
                wait("T", "dve", m["nrmz", B - 1, 7])
                wait("T", "dve", m["porj_ev", B - 1, 7, KD - 1])
            T(lambda e: e.matmul(ps_s0[:, 0:512], onesrowf_sb[:],
                                 statrow_sb[0:1, 0, :], start=True, stop=True))
            inst = T(lambda e: e.matmul(ps_s0[:, 512:1024], onesrowf_sb[:],
                                        statrow_sb[0:1, 3, :], start=True, stop=True))
            m["ln1bc", s] = step("pe", inst)
            wait("V", "pe", m["ln1bc", s])
            for k in range(KD):
                V(lambda e, k=k: e.tensor_tensor(r3(TMPF, KD, 512)[:, k, :],
                                                 r3(XS, KD, 512)[:, k, :],
                                                 ps_s0[:, 0:512], ALU.subtract))
            inst = None
            for k in range(KD):
                inst = V(lambda e, k=k, s=s: e.tensor_tensor(
                    r3(X1, KD, 2048)[:, k, s * 512:(s + 1) * 512],
                    r3(TMPF, KD, 512)[:, k, :], ps_s0[:, 512:1024], ALU.mult))
            m["ln1ap", s] = step("dve", inst)
            if s == 3:
                dump("XS", XS, [("dve", m["opj_ev", 3, KD - 1])])
                dump("ATO3", ATO[1], [])
        m["x1done"] = m["ln1ap", 3]

        wait("G", "dve", m["x1done"])
        wait("G", "pe", m["ln1mm", 3])
        wait("G", "act", m["ln1sq", 3])
        step("gw", G(lambda e: e.dma_start(out=r3(WGS, KD, 768),
                                           in_=wg_v[:, :, 0:768])), 16)
        inst = G(lambda e: e.dma_start(out=r3(WUS, KD, 768), in_=wu_v[:, :, 0:768]))
        m["wgu", 0] = step("gw", inst, 16)
        m["wgu_h", 0] = m["wgu", 0]
        inst = G(lambda e: e.dma_start(out=r3(WDS, 6, 1024), in_=wd_v[:, 0:6, :]))
        m["wd", 0] = step("gd", inst, 16)

        for hh in range(2):
            for sup in range(4):
                su = hh * 4 + sup
                for s2 in range(2):
                    tok = hh * 1024 + s2 * 512
                    for i in range(6):
                        pg = [ps_s0, ps_s1][i % 2]
                        wait("T", "gw", m["wgu_h", su] if i < 3 else m["wgu", su])
                        if ("ffn_hp", su, s2, i - 2) in m:
                            wait("T", "dve", m["ffn_hp", su, s2, i - 2])
                        elif ("ffn_hp", su, s2 - 1, i + 4) in m:
                            wait("T", "dve", m["ffn_hp", su, s2 - 1, i + 4])
                        elif su > 0 and ("ffn_hp", su - 1, 1, i + 4) in m:
                            wait("T", "dve", m["ffn_hp", su - 1, 1, i + 4])
                        if su == 0 and s2 == 0 and i == 0:
                            wait("T", "dve", m["x1done"])
                        if sup == 0 and s2 == 0 and i == 0 and hh > 0:
                            wait("T", "dve", m["ln2ap", hh - 1, 1])
                        inst = None
                        for k in range(KD):
                            T(lambda e, k=k, i=i, pg=pg, tok=tok: e.matmul(
                                pg[:, 0:512],
                                WGS[:, k * 768 + i * 128: k * 768 + (i + 1) * 128],
                                r3(X1, KD, 2048)[:, k, tok:tok + 512],
                                start=(k == 0), stop=(k == KD - 1)))
                        for k in range(KD):
                            inst = T(lambda e, k=k, i=i, pg=pg, tok=tok: e.matmul(
                                pg[:, 512:1024],
                                WUS[:, k * 768 + i * 128: k * 768 + (i + 1) * 128],
                                r3(X1, KD, 2048)[:, k, tok:tok + 512],
                                start=(k == 0), stop=(k == KD - 1)))
                        m["ffn_gu", su, s2, i] = step("pe", inst)
                        wait("A", "pe", m["ffn_gu", su, s2, i])
                        if ("ffn_hp", su, s2, i - 2) in m:
                            wait("A", "dve", m["ffn_hp", su, s2, i - 2])
                        elif ("ffn_hp", su, s2 - 1, i + 4) in m:
                            wait("A", "dve", m["ffn_hp", su, s2 - 1, i + 4])
                        elif su > 0 and ("ffn_hp", su - 1, 1, i + 4) in m:
                            wait("A", "dve", m["ffn_hp", su - 1, 1, i + 4])
                        inst = A(lambda e, i=i, pg=pg: e.activation(
                            silu_sb[:, i % 2, :], pg[:, 0:512], AF.Silu, scale=FSC))
                        m["ffn_silu", su, s2, i] = step("act", inst)
                        wait("V", "act", m["ffn_silu", su, s2, i])
                        inst = V(lambda e, i=i, pg=pg: e.tensor_tensor(
                            r3(HB, 6, 512)[:, i, :], silu_sb[:, i % 2, :],
                            pg[:, 512:1024], ALU.mult))
                        m["ffn_hp", su, s2, i] = step("dve", inst)

                    wait("T", "gd", m["wd", su])
                    for f in range(KD):
                        pd = [ps_a0, ps_a1][f % 2]
                        if ("ffn_dev", su, s2, f - 2) in m:
                            wait("T", "dve", m["ffn_dev", su, s2, f - 2])
                        elif ("ffn_dev", su, s2 - 1, f + 2) in m:
                            wait("T", "dve", m["ffn_dev", su, s2 - 1, f + 2])
                        elif su > 0 and ("ffn_dev", su - 1, 1, f + 2) in m:
                            wait("T", "dve", m["ffn_dev", su - 1, 1, f + 2])
                        if f == 0:
                            wait("T", "dve", m["ffn_hp", su, s2, 5])
                        inst = None
                        for i in range(6):
                            inst = T(lambda e, i=i, f=f, pd=pd: e.matmul(
                                pd[:, 0:512],
                                r3(WDS, 6, 1024)[:, i, f * 128:(f + 1) * 128],
                                r3(HB, 6, 512)[:, i, :],
                                start=(i == 0), stop=(i == 5)))
                        m["ffn_d", su, s2, f] = step("pe", inst)
                        wait("V", "pe", m["ffn_d", su, s2, f])
                        if sup == 0 and hh == 1:
                            wait("V", "so", m["sodone", 0])
                        if sup == 0:
                            inst = V(lambda e, f=f, s2=s2, hh=hh, pd=pd:
                                     e.scalar_tensor_tensor(
                                r3(DACC, KD, 1024)[:, f, s2 * 512:(s2 + 1) * 512],
                                pd[:, 0:512], FSC * FSC,
                                r3(X1, KD, 2048)[:, f,
                                                 hh * 1024 + s2 * 512:
                                                 hh * 1024 + (s2 + 1) * 512],
                                ALU.mult, ALU.add))
                        else:
                            inst = V(lambda e, f=f, s2=s2, pd=pd:
                                     e.scalar_tensor_tensor(
                                r3(DACC, KD, 1024)[:, f, s2 * 512:(s2 + 1) * 512],
                                pd[:, 0:512], FSC * FSC,
                                r3(DACC, KD, 1024)[:, f, s2 * 512:(s2 + 1) * 512],
                                ALU.mult, ALU.add))
                        m["ffn_dev", su, s2, f] = step("dve", inst)

                if su < 7:
                    nsup = (su + 1) % 4
                    wait("G", "dve", m["ffn_hp", su, 1, 2])
                    step("gw", G(lambda e, nsup=nsup: e.dma_start(
                        out=r3(WGS, KD, 768)[:, :, 0:384],
                        in_=wg_v[:, :, nsup * 768:nsup * 768 + 384])), 16)
                    inst = G(lambda e, nsup=nsup: e.dma_start(
                        out=r3(WUS, KD, 768)[:, :, 0:384],
                        in_=wu_v[:, :, nsup * 768:nsup * 768 + 384]))
                    m["wgu_h", su + 1] = step("gw", inst, 16)
                    wait("G", "dve", m["ffn_hp", su, 1, 5])
                    step("gw", G(lambda e, nsup=nsup: e.dma_start(
                        out=r3(WGS, KD, 768)[:, :, 384:768],
                        in_=wg_v[:, :, nsup * 768 + 384:(nsup + 1) * 768])), 16)
                    inst = G(lambda e, nsup=nsup: e.dma_start(
                        out=r3(WUS, KD, 768)[:, :, 384:768],
                        in_=wu_v[:, :, nsup * 768 + 384:(nsup + 1) * 768]))
                    m["wgu", su + 1] = step("gw", inst, 16)
                    wait("G", "dve", m["ffn_dev", su, 1, KD - 1])
                    inst = G(lambda e, nsup=nsup: e.dma_start(
                        out=r3(WDS, 6, 1024),
                        in_=wd_v[:, nsup * 6:(nsup + 1) * 6, :]))
                    m["wd", su + 1] = step("gd", inst, 16)

            lastsu = hh * 4 + 3
            dv = r3(DACC, KD, 1024)
            if hh == 0:
                dump("DACC0", DACC, [("dve", m["ffn_dev", lastsu, 1, KD - 1])])
                dump("HB0", HB, [])
            for s2 in range(2):
                wait("T", "dve", m["ffn_dev", lastsu, 1, KD - 1])
                if ("ln2ap", hh, s2 - 1) in m:
                    wait("T", "dve", m["ln2ap", hh, s2 - 1])
                elif hh > 0 and ("ln2ap", hh - 1, 1) in m:
                    wait("T", "dve", m["ln2ap", hh - 1, 1])
                for k in range(KD):
                    T(lambda e, k=k, s2=s2: e.matmul(
                        ps_a0[0:1, 0:512], onesf_sb[:],
                        dv[:, k, s2 * 512:(s2 + 1) * 512],
                        start=(k == 0), stop=(k == KD - 1)))
                wait("A", "dve", m["ffn_dev", lastsu, s2, KD - 1])
                if ("ln2mm", hh, s2 - 1) in m:
                    wait("A", "pe", m["ln2mm", hh, s2 - 1])
                elif hh > 0 and ("ln2mm", hh - 1, 1) in m:
                    wait("A", "pe", m["ln2mm", hh - 1, 1])
                inst = A(lambda e, s2=s2: e.activation(
                    r3(XSQ2, KD, 512)[:, :, :], dv[:, :, s2 * 512:(s2 + 1) * 512],
                    AF.Square))
                m["ln2sq", hh, s2] = step("act", inst)
                wait("T", "act", m["ln2sq", hh, s2])
                inst = None
                for k in range(KD):
                    inst = T(lambda e, k=k: e.matmul(
                        ps_a1[0:1, 0:512], onesb_sb[:], r3(XSQ2, KD, 512)[:, k, :],
                        start=(k == 0), stop=(k == KD - 1)))
                m["ln2mm", hh, s2] = step("pe", inst)
                wait("V", "pe", m["ln2mm", hh, s2])
                V(lambda e: e.tensor_scalar(statrow_sb[0:1, 0, :], ps_a0[0:1, 0:512],
                                            1.0 / D, None, ALU.mult))
                V(lambda e: e.tensor_scalar(statrow_sb[0:1, 1, :], ps_a1[0:1, 0:512],
                                            1.0 / D, 1e-5, ALU.mult, ALU.add))
                inst = V(lambda e: e.tensor_tensor(statrow_sb[0:1, 2, :],
                                            statrow_sb[0:1, 0, :],
                                            statrow_sb[0:1, 0, :], ALU.mult))
                _sq2 = step("dve", inst)
                wait("V", "dve", _sq2)
                inst = V(lambda e: e.tensor_tensor(statrow_sb[0:1, 1, :],
                                                   statrow_sb[0:1, 1, :],
                                                   statrow_sb[0:1, 2, :], ALU.subtract))
                m["ln2var", hh, s2] = step("dve", inst)
                if hh == 0 and s2 == 0:
                    dump("STAT", statrow_sb[0:1, :, :].rearrange("p a b -> p (a b)"),
                         [("dve", m["ln2var", 0, 0])])
                wait("A", "dve", m["ln2var", hh, s2])
                inst = A(lambda e: e.activation(statrow_sb[0:1, 2, :],
                                                statrow_sb[0:1, 1, :], AF.Ln))
                _ln2 = step("act", inst)
                wait("A", "act", _ln2)
                inst = A(lambda e: e.activation(statrow_sb[0:1, 3, :],
                                                statrow_sb[0:1, 2, :],
                                                AF.Exp, scale=-0.5))
                m["ln2rstd", hh, s2] = step("act", inst)
                wait("T", "act", m["ln2rstd", hh, s2])
                T(lambda e: e.matmul(ps_s0[:, 0:512], onesrowf_sb[:],
                                     statrow_sb[0:1, 0, :], start=True, stop=True))
                inst = T(lambda e: e.matmul(ps_s0[:, 512:1024], onesrowf_sb[:],
                                            statrow_sb[0:1, 3, :],
                                            start=True, stop=True))
                m["ln2bc", hh, s2] = step("pe", inst)
                wait("V", "pe", m["ln2bc", hh, s2])
                for k in range(KD):
                    V(lambda e, k=k, s2=s2: e.tensor_tensor(
                        r3(TMPF2, KD, 512)[:, k, :], dv[:, k, s2 * 512:(s2 + 1) * 512],
                        ps_s0[:, 0:512], ALU.subtract))
                inst = None
                for k in range(KD):
                    inst = V(lambda e, k=k, s2=s2: e.tensor_tensor(
                        dv[:, k, s2 * 512:(s2 + 1) * 512], r3(TMPF2, KD, 512)[:, k, :],
                        ps_s0[:, 512:1024], ALU.mult))
                m["ln2ap", hh, s2] = step("dve", inst)
                wait("S", "dve", m["ln2ap", hh, s2])
                inst = S(lambda e, hh=hh, s2=s2: e.dma_start(
                    out=out_v[:, :, hh * 1024 + s2 * 512: hh * 1024 + (s2 + 1) * 512],
                    in_=dv[:, :, s2 * 512:(s2 + 1) * 512]))
                m["store", hh, s2] = step("so", inst, 16)
            m["sodone", hh] = c["so"]

        if DEBUG:
            wait("S", "so", m["sodone", 1])
            wait("S", "dve", c["dve"])
            wait("S", "act", c["act"])
            wait("S", "pe", c["pe"])
            wait("S", "dbg", c["dbg"])
            for nm, sb in [("X1", X1), ("DACC", DACC), ("HB", HB),
                           ("RS", runsum_sb[:].rearrange("p h q -> p (h q)")),
                           ("CR", crow_sb[:])]:
                step("so", S(lambda e, nm=nm, sb=sb: e.dma_start(
                    out=dbg[nm], in_=sb)), 16)
            wait("S", "so", c["so"])
        else:
            wait("S", "so", m["sodone", 1])
        return c

    with nc.Block() as block:
        @block.tensor
        def _(e):
            emit("T", e)

        @block.scalar
        def _(e):
            emit("A", e)

        @block.vector
        def _(e):
            emit("V", e)

        @block.gpsimd
        def _(e):
            emit("G", e)

        @block.sync
        def _(e):
            emit("S", e)

    stack.close()
    return nc


_NC_CACHE = None


def kernel(L0_attn_x, L0_attn_w_qkv, L0_attn_b_qkv, L0_attn_w_out, L0_attn_b_out,
           L0_ffn_w_gate, L0_ffn_w_up, L0_ffn_w_down):
    global _NC_CACHE
    x = np.asarray(L0_attn_x, dtype=np.float32)
    wqkv_h = np.asarray(L0_attn_w_qkv, dtype=np.float32)
    bqkv_h = np.asarray(L0_attn_b_qkv, dtype=np.float32)
    w_out = np.asarray(L0_attn_w_out, dtype=np.float32)
    b_out = np.asarray(L0_attn_b_out, dtype=np.float32)
    wg_h = np.asarray(L0_ffn_w_gate, dtype=np.int32)
    wu_h = np.asarray(L0_ffn_w_up, dtype=np.int32)
    wd_h = np.asarray(L0_ffn_w_down, dtype=np.int32)

    xT = np.ascontiguousarray(x.reshape(M, D).T)
    woutT = w_out.T
    wgT = np.ascontiguousarray(wg_h.T)
    wuT = np.ascontiguousarray(wu_h.T)
    wdT = np.ascontiguousarray(wd_h.T)
    boutr = np.ascontiguousarray(b_out.reshape(KD, 128).T)

    in_maps = []
    for cid in range(NCORE):
        q0 = 2 * cid * 64
        rows = np.concatenate([np.arange(q0, q0 + 128),
                               np.arange(D + q0, D + q0 + 128),
                               np.arange(2 * D + q0, 2 * D + q0 + 128)])
        wqkv_c = np.ascontiguousarray(wqkv_h[rows].T)
        bq = bqkv_h[q0:q0 + 128]
        bk = bqkv_h[D + q0:D + q0 + 128]
        bv = bqkv_h[2 * D + q0:2 * D + q0 + 128]
        bqkv_c = np.ascontiguousarray(np.stack([bq, bk, bv], axis=1))
        strip_cols = np.concatenate(
            [b * N + cid * 512 + np.arange(512) for b in range(B)])
        xq_c = np.ascontiguousarray(xT[:, strip_cols])
        woutc_c = np.ascontiguousarray(woutT[cid * 128:(cid + 1) * 128, :])
        in_maps.append({
            "xg": xT, "xq": xq_c, "wqkv": wqkv_c, "bqkv": bqkv_c,
            "woutc": woutc_c, "bout": boutr, "wg": wgT, "wu": wuT, "wd": wdT,
        })

    if _NC_CACHE is None:
        _NC_CACHE = build_nc()
    res = run_bass_kernel_spmd(_NC_CACHE, in_maps, core_ids=list(range(NCORE)))
    globals()["_LAST_RES"] = res
    outbuf = np.empty((M, D), np.float32)
    for cid in range(NCORE):
        oc = np.asarray(res.results[cid]["out"])
        for b in range(B):
            outbuf[b * N + cid * 512: b * N + (cid + 1) * 512] = \
                oc[:, b * 512:(b + 1) * 512].T
    return outbuf.reshape(B, N, D)
